# version 04
# baseline (speedup 1.0000x reference)
import numpy as np
from contextlib import ExitStack

import concourse.bass as bass
import concourse.mybir as mybir
import concourse.bass_utils as _bass_utils
from concourse.bass_utils import run_bass_kernel_spmd

_orig_run_command = _bass_utils.run_command


def _run_command_no_birverifier(argv, **kwargs):
    argv = [a.replace("birverifier,", "") if isinstance(a, str) else a
            for a in argv]
    return _orig_run_command(argv, **kwargs)


_bass_utils.run_command = _run_command_no_birverifier

F32 = mybir.dt.float32
F32R = mybir.dt.float32r
AF = mybir.ActivationFunctionType
ALU = mybir.AluOpType

E = 2048
E2 = 4096
DF = 8192
HD = 128
OWN = 1024
HALO = 128
T = OWN + HALO
KD = E // 128
KC = E2 // 128
KE = DF // 128
NG = 3
GT = T // NG
NH = 2
HT = OWN // NH
LN_EPS = 1e-5
RMS_EPS = 1e-6
Q_EPS = 1e-8


def build_nc(tap=None, serial_dma=False, race_detect=True):
    nc = bass.Bass("TRN2", target_bir_lowering=False,
                   detect_race_conditions=race_detect)

    xwin = nc.dram_tensor("xwin", [T, E], F32, kind="ExternalInput").ap()
    maskr = nc.dram_tensor("maskr", [128, GT], F32, kind="ExternalInput").ap()
    cresh = nc.dram_tensor("cresh", [128, KD], F32, kind="ExternalInput").ap()
    wq1 = nc.dram_tensor("wq1", [E, HD], F32, kind="ExternalInput").ap()
    wk1 = nc.dram_tensor("wk1", [E, HD], F32, kind="ExternalInput").ap()
    wq2 = nc.dram_tensor("wq2", [HD, E2], F32, kind="ExternalInput").ap()
    wk2 = nc.dram_tensor("wk2", [HD, E2], F32, kind="ExternalInput").ap()
    wvr = nc.dram_tensor("wvr", [KC, KD, 128, 128], F32, kind="ExternalInput").ap()
    wor = nc.dram_tensor("wor", [KD, KC, 128, 128], F32, kind="ExternalInput").ap()
    wupr = nc.dram_tensor("wupr", [KE, KD, 128, 128], F32, kind="ExternalInput").ap()
    wgater = nc.dram_tensor("wgater", [KE, KD, 128, 128], F32, kind="ExternalInput").ap()
    wdownr = nc.dram_tensor("wdownr", [KE, KD, 128, 128], F32, kind="ExternalInput").ap()
    wmodr = nc.dram_tensor("wmodr", [24, KD, 128, 512], F32, kind="ExternalInput").ap()
    normwr = nc.dram_tensor("normwr", [128, KC], F32, kind="ExternalInput").ap()
    idt_in = nc.dram_tensor("idt_in", [128, 128], F32, kind="ExternalInput").ap()
    xout = nc.dram_tensor("xout", [OWN, E], F32, kind="ExternalOutput").ap()
    x1s = nc.dram_tensor("x1s", [128, KD * OWN], F32).ap()

    with ExitStack() as ctx:
        en = ctx.enter_context

        A1 = en(nc.sbuf_tensor("A1", [128, 12288], F32R))
        A2 = en(nc.sbuf_tensor("A2", [128, 12288], F32R))
        A3 = en(nc.sbuf_tensor("A3", [128, 8192], F32R))
        A4 = en(nc.sbuf_tensor("A4", [128, 8192], F32))
        A5 = en(nc.sbuf_tensor("A5", [128, 5120], F32R))
        A6 = en(nc.sbuf_tensor("A6", [128, 4608], F32R))
        A7 = en(nc.sbuf_tensor("A7", [128, 1536], F32))
        MKC = en(nc.sbuf_tensor("MKC", [128, 640], F32))

        def fv(t, a, b):
            return t[:, a:b].bitcast(F32)

        def QH(cc, n):
            return fv(A1, cc * n, (cc + 1) * n)

        def SQSC(k):
            return fv(A1, k * GT, (k + 1) * GT)

        def H1R(k):
            return A2[:, k * GT:(k + 1) * GT]

        def H1LO(k):
            return A2[:, KD * GT + k * GT:KD * GT + (k + 1) * GT]

        def XFM(k):
            return fv(A3, k * GT, (k + 1) * GT)
        RSTD = fv(A3, 6144, 6144 + GT)

        def X1T(par):
            return fv(A3, 6528 + par * GT, 6528 + (par + 1) * GT)

        def WVRAW(p):
            return A4[:, p * 2048:(p + 1) * 2048]
        XTMP = A4[:, 4096:6144]

        def WOS(p):
            return A4[:, p * 4096:(p + 1) * 4096]

        def WVHr(hh):
            return A5[:, hh * 2048:hh * 2048 + 1024]

        def WVHlo(hh):
            return A5[:, hh * 2048 + 1024:(hh + 1) * 2048]

        def W2Rv(p):
            return A5[:, 4096 + p * 512:4096 + p * 512 + 256]

        def W2LOv(p):
            return A5[:, 4096 + p * 512 + 256:4096 + (p + 1) * 512]

        def W1Rv(k):
            return A5[:, k * 128:(k + 1) * 128]

        def W1LOv(k):
            return A5[:, 2048 + k * 128:2048 + (k + 1) * 128]

        G1QR = A6[:, 0:GT]
        G1QLO = A6[:, GT:2 * GT]
        G1KR = A6[:, 2 * GT:3 * GT]
        G1KLO = A6[:, 3 * GT:4 * GT]
        FB = fv(A6, 4 * GT, 5 * GT)
        AB = fv(A6, 5 * GT, 6 * GT)
        KB = fv(A6, 6 * GT, 7 * GT)
        BB = fv(A6, 7 * GT, 8 * GT)
        HB = fv(A6, 8 * GT, 9 * GT)
        SQB = fv(A6, 9 * GT, 10 * GT)
        ISTD = FB
        MISTD = AB
        W1TMP = fv(A6, 3840, 3968)
        W2TMP = fv(A6, 3968, 4224)
        CRE = fv(A6, 4224, 4240)
        SC_ = fv(A6, 4240, 4256)
        MOD = fv(A6, 4256, 4352)
        NW = fv(A6, 4352, 4384)
        HCAR = fv(A6, 4384, 4416)
        SCOL = fv(A6, 4416, 4432)
        ABSC = fv(A6, 4432, 4444)
        GBS = fv(A6, 4444, 4460)
        ROWT = fv(A6, 4460, 4464)
        TT1 = fv(A6, 4464, 4480)
        TT2 = fv(A6, 4480, 4496)
        TT3 = fv(A6, 4496, 4512)
        TT4 = fv(A6, 4512, 4528)
        TT5 = fv(A6, 4528, 4544)
        TT6 = fv(A6, 4544, 4560)
        R0 = A7[0:1, 0:512]
        R1 = A7[0:1, 512:1024]
        R2 = A7[0:1, 1024:1536]
        MASK = fv(MKC, 0, GT)
        IDT = MKC[:, 384:512]
        ONES = MKC[:, 512:640]
        def H2R(k):
            return A2[:, k * HT:(k + 1) * HT]

        def H2SQ(k):
            return fv(A1, k * HT, (k + 1) * HT)

        def H2LO(k):
            return A1[:, k * HT:(k + 1) * HT]

        def X2H(k):
            return fv(A3, k * HT, (k + 1) * HT)

        def HIDR(j):
            return A3[:, j * 1024:j * 1024 + 512]

        def HIDLO(j):
            return A3[:, j * 1024 + 512:(j + 1) * 1024]

        def TUG(m):
            return A3[:, 4096 + m * 2048:4096 + (m + 1) * 2048]

        def DACC(mc):
            return A4[:, mc * HT:(mc + 1) * HT]

        def TDS(j):
            return [A5[:, 0:2048], A5[:, 2048:4096],
                    A1[:, 9216:11264], A2[:, 9216:11264]][j]
        GTSC = fv(A1, 8192, 9216)
        LTSC = fv(A2, 8192, 9216)
        H2T1 = fv(A2, 11264, 11776)
        WFL = fv(A6, 0, 2048)
        SGB = fv(A6, 2048, 2560)
        T1B = fv(A6, 2560, 3072)
        HPL = fv(A6, 3072, 3584)
        XO = fv(A6, 0, 2048)
        I2 = A4[:, 0:512]
        M2 = A4[:, 512:1024]

        ps = [en(nc.psum_tensor(f"ps{i}", [128, 512], F32)) for i in range(8)]

        hs = {}
        for s in ["DLX", "DLV0", "DLV1", "DL2", "DLWA", "DLWB", "DLP", "DLF",
                  "TV", "VV", "SC", "STA", "STB", "STO"]:
            hs[s] = en(nc.semaphore(s))
        C = {k: 0 for k in hs}

        block = en(nc.Block())
        STR = {"sync": [], "tensor": [], "vector": [], "scalar": []}

        def wait(stream, sem, val):
            if val and val > 0:
                STR[stream].append(lambda e, h=hs[sem], v=val: e.wait_ge(h, v))

        def wait2(stream, *svs):
            best = {}
            for sv in svs:
                if sv is None:
                    continue
                s, v = sv
                if v and v > 0:
                    best[s] = max(best.get(s, 0), v)
            for s, v in best.items():
                wait(stream, s, v)

        def dma(dst, src, sem="DLX"):
            C[sem] += 16
            STR["sync"].append(lambda e, d=dst, s=src, h=hs[sem]:
                               e.dma_start(out=d, in_=s).then_inc(h, 16))
            if serial_dma:
                wait("sync", sem, C[sem])
            return C[sem]

        def mm(psap, lhsT, rhs, start, stop, inc=None):
            if inc is None:
                STR["tensor"].append(lambda e, o=psap, a=lhsT, b=rhs, s1=start,
                                     s2=stop: e.matmul(o, a, b, start=s1, stop=s2))
                return None
            C[inc] += 1
            STR["tensor"].append(lambda e, o=psap, a=lhsT, b=rhs, s1=start, s2=stop,
                                 h=hs[inc]: e.matmul(o, a, b, start=s1,
                                                     stop=s2).then_inc(h, 1))
            return C[inc]

        def tr(psap, inap, inc=None):
            if inc is None:
                STR["tensor"].append(lambda e, o=psap, a=inap: e.transpose(o, a, IDT))
                return None
            C[inc] += 1
            STR["tensor"].append(lambda e, o=psap, a=inap, h=hs[inc]:
                                 e.transpose(o, a, IDT).then_inc(h, 1))
            return C[inc]

        def vop(fn, inc=None):
            if inc is None:
                STR["vector"].append(fn)
                return None
            C[inc] += 1
            STR["vector"].append(lambda e, f=fn, h=hs[inc]: f(e).then_inc(h, 1))
            return C[inc]

        def sop(fn, inc=None):
            if inc is None:
                STR["scalar"].append(fn)
                return None
            C[inc] += 1
            STR["scalar"].append(lambda e, f=fn, h=hs[inc]: f(e).then_inc(h, 1))
            return C[inc]

        def vdrain():
            STR["vector"].append(lambda e: e.drain())

        bank_free = [("VV", 0)] * 8

        def use_bank(i):
            wait2("tensor", bank_free[i])

        def finish(extra_src=None, extra_n=0):
            if extra_src is not None:
                xv = xout.rearrange("(p a) d -> p (a d)", p=128)
                dma(xv[:, 0:extra_n], extra_src, sem="STO")
            for st_ in ("STA", "STB", "STO"):
                wait("sync", st_, C[st_])
            emit_all(block, STR)

        dma(IDT, idt_in)
        dma(CRE, cresh)
        dma(NW, normwr)
        d0 = dma(MASK, maskr)
        wait("vector", "DLX", d0)
        vop(lambda e: e.memset(ONES, 1.0))
        vop(lambda e: e.memset(HCAR, 0.0))
        v_ones = vop(lambda e: e.memset(ABSC, 0.0), inc="VV")
        wait("scalar", "DLX", d0)
        wait("scalar", "VV", v_ones)
        c_sg = sop(lambda e: e.activation(SC_, CRE, AF.Sigmoid), inc="SC")
        wait("vector", "SC", c_sg)
        v_sc = vop(lambda e: e.tensor_mul(SC_, SC_, CRE), inc="VV")

        wait("tensor", "VV", v_sc)
        wait("tensor", "DLX", d0)
        mod_free = [("TV", 0), ("TV", 0)]
        r = None
        for jt in range(24):
            for half in range(2):
                p = half
                wait2("sync", mod_free[p])
                dd = dma(WOS(p).rearrange("p (a n) -> p a n", n=512),
                         wmodr[jt, half * 8:(half + 1) * 8].rearrange("a p n -> p a n"),
                         sem="DLWA" if p == 0 else "DLWB")
                wait("tensor", "DLWA" if p == 0 else "DLWB", dd)
                if half == 0:
                    use_bank(6)
                for kk in range(8):
                    k = half * 8 + kk
                    r = mm(ps[6][0:1, 0:512], SC_[:, k:k + 1],
                           WOS(p)[:, kk * 512:(kk + 1) * 512],
                           start=(k == 0), stop=(k == KD - 1),
                           inc="TV" if kk == 7 else None)
                mod_free[p] = ("TV", r)
            wait("vector", "TV", r)
            ev = vop(lambda e: e.tensor_copy(R0, ps[6][0:1, 0:512]), inc="VV")
            bank_free[6] = ("VV", ev)
            wait("tensor", "VV", ev)
            for u in range(4):
                j = jt * 4 + u
                use_bank(7)
                r2 = mm(ps[7][:, 0:1], R0[0:1, u * 128:(u + 1) * 128],
                        ONES[0:1, 0:1], start=True, stop=True, inc="TV")
                wait("vector", "TV", r2)
                ev2 = vop(lambda e, j=j: e.tensor_copy(MOD[:, j:j + 1],
                                                       ps[7][:, 0:1]), inc="VV")
                bank_free[7] = ("VV", ev2)
        vop(lambda e: e.tensor_scalar_add(MOD[:, 16:32], MOD[:, 16:32], 1.0))
        vop(lambda e: e.tensor_scalar_add(MOD[:, 64:80], MOD[:, 64:80], 1.0))

        ga_ = MOD[:, 32:48]
        gb_ = MOD[:, 80:96]
        vop(lambda e: e.tensor_scalar_max(TT1, ga_, 0.25))
        vop(lambda e: e.tensor_scalar_max(TT4, gb_, 0.25))
        vop(lambda e: e.tensor_scalar_min(TT2, ga_, -0.25))
        vop(lambda e: e.tensor_scalar_min(TT5, gb_, -0.25))
        vop(lambda e: e.tensor_single_scalar(TT3, ga_, 0.0, ALU.is_ge))
        vop(lambda e: e.tensor_single_scalar(TT6, gb_, 0.0, ALU.is_ge))
        vop(lambda e: e.tensor_sub(TT1, TT1, TT2))
        vop(lambda e: e.tensor_sub(TT4, TT4, TT5))
        vop(lambda e: e.tensor_mul(TT1, TT1, TT3))
        vop(lambda e: e.tensor_mul(TT4, TT4, TT6))
        vop(lambda e: e.tensor_add(ga_, TT1, TT2))
        vop(lambda e: e.tensor_add(gb_, TT4, TT5))
        v_mod = vop(lambda e: e.engine_nop(), inc="VV")
        if tap == "mod":
            wait("sync", "VV", v_mod)
            finish(MOD, 96)
            return nc

        prep_state = {"i": 0, "free": ("TV", 0), "vv": 0}
        prep_mats = [wupr, wgater, wdownr]

        def emit_prepass(n):
            for _ in range(n):
                i = prep_state["i"]
                if i >= 3 * KE:
                    return
                prep_state["i"] += 1
                mi, ec = i // KE, i % KE
                wait2("sync", prep_state["free"])
                dd = dma(XTMP.rearrange("p (a n) -> p a n", n=128),
                         prep_mats[mi][ec].rearrange("a p n -> p a n"), sem="DLP")
                wait("scalar", "DLP", dd)
                if prep_state["vv"]:
                    wait("scalar", "VV", prep_state["vv"])
                wait2("scalar", bank_free[7])
                sr = None
                for u in range(4):
                    sr = sop(lambda e, u=u: e.activation(
                        ps[7][:, 0:512], XTMP[:, u * 512:(u + 1) * 512], AF.Abs,
                        accum_out=ROWT[:, u:u + 1]),
                        inc="SC" if u == 3 else None)
                wait("vector", "SC", sr)
                av = None
                for u in range(4):
                    av = vop(lambda e, mi=mi, u=u: e.tensor_add(
                        ABSC[:, mi * 4 + u:mi * 4 + u + 1],
                        ABSC[:, mi * 4 + u:mi * 4 + u + 1], ROWT[:, u:u + 1]),
                        inc="VV" if u == 3 else None)
                prep_state["free"] = ("SC", sr)
                prep_state["vv"] = av
                bank_free[7] = ("SC", sr)

        def emit_scales():
            wait("tensor", "VV", prep_state["vv"])
            for m_ in range(3):
                use_bank(7)
                r = mm(ps[7][0:1, 0:4], ONES[:, 0:1], ABSC[:, m_ * 4:(m_ + 1) * 4],
                       start=True, stop=True, inc="TV")
                wait("vector", "TV", r)
                ev = vop(lambda e, m_=m_: e.tensor_copy(R2[0:1, m_ * 4:m_ * 4 + 4],
                                                        ps[7][0:1, 0:4]), inc="VV")
                bank_free[7] = ("VV", ev)
            vdrain()
            for m_ in range(3):
                vop(lambda e, m_=m_: e.tensor_reduce(
                    R1[0:1, m_:m_ + 1], R2[0:1, m_ * 4:m_ * 4 + 4],
                    mybir.AxisListType.X, ALU.add))
            vdrain()
            vop(lambda e: e.tensor_scalar(R1[0:1, 0:3], R1[0:1, 0:3],
                                          1.0 / (DF * E), Q_EPS, ALU.mult, ALU.add))
            vdrain()
            vop(lambda e: e.tensor_scalar_mul(R1[0:1, 3:6], R1[0:1, 0:3], 0.5))
            vdrain()
            vop(lambda e: e.tensor_scalar_mul(R1[0:1, 6:9], R1[0:1, 3:6], -1.0))
            vop(lambda e: e.tensor_mul(R1[0:1, 9:10], R1[0:1, 0:1], R1[0:1, 1:2]))
            vdrain()
            vr = vop(lambda e: e.tensor_mul(R1[0:1, 9:10], R1[0:1, 9:10],
                                            R1[0:1, 2:3]), inc="VV")
            wait("tensor", "VV", vr)
            use_bank(7)
            r = mm(ps[7][:, 0:10], ONES[0:1, :], R1[0:1, 0:10],
                   start=True, stop=True, inc="TV")
            wait("vector", "TV", r)
            vop(lambda e: e.tensor_copy(SCOL[:, 0:10], ps[7][:, 0:10]))
            vdrain()
            gv = vop(lambda e: e.tensor_scalar(GBS, MOD[:, 80:96], SCOL[:, 9:10],
                                               None, ALU.mult), inc="VV")
            bank_free[7] = ("VV", gv)
            return gv

        xw_t = xwin.rearrange("(a p) d -> a p d", p=128)
        w2c_q = wq2.rearrange("p (c n) -> c p n", n=128)
        w2c_k = wk2.rearrange("p (c n) -> c p n", n=128)
        wq1t = wq1.rearrange("(k p) m -> k p m", p=128)
        wk1t = wk1.rearrange("(k p) m -> k p m", p=128)
        xtmp_free = mod_free[1]
        x1t_free = [("VV", 0), ("VV", 0)]
        w1tmp_free = ("VV", 0)
        w2tmp_free = [("VV", 0)]
        w2p_free = [("TV", 0), ("TV", 0)]
        wvh_free = [("TV", 0), ("TV", 0)]
        wvraw_free = [mod_free[0], mod_free[1]]
        wos_free = [("TV", 0), ("TV", 0)]
        rwo = None
        for g in range(NG):
            own_lo = HALO if g == 0 else 0
            n_own = GT - own_lo
            own_glob = g * GT + own_lo - HALO
            last_ev = 0
            rlast = None
            for tt in range(GT // 128):
                wait2("sync", xtmp_free, prep_state["free"])
                dd = dma(XTMP, xw_t[(g * GT) // 128 + tt], sem="DLP")
                wait("tensor", "DLP", dd)
                if prep_state["vv"]:
                    wait("tensor", "SC", prep_state["free"][1]
                         if prep_state["free"][0] == "SC" else 0)
                for k in range(KD):
                    bank = 2 + (k % 2)
                    use_bank(bank)
                    rlast = tr(ps[bank][:, 0:128], XTMP[:, k * 128:(k + 1) * 128],
                               inc="TV")
                    wait("vector", "TV", rlast)
                    last_ev = vop(lambda e, k=k, tt=tt, b=bank: e.tensor_copy(
                        XFM(k)[:, tt * 128:(tt + 1) * 128], ps[b][:, 0:128]),
                        inc="VV")
                    bank_free[bank] = ("VV", last_ev)
                xtmp_free = ("TV", rlast)
            wait("scalar", "VV", last_ev)
            sq_done = None
            for k in range(KD):
                sq_done = sop(lambda e, k=k: e.activation(
                    SQSC(k), XFM(k), AF.Square), inc="SC" if k == KD - 1 else None)
            wait("tensor", "SC", sq_done)
            use_bank(4)
            use_bank(5)
            r = None
            for k in range(KD):
                mm(ps[4][0:1, 0:GT], ONES[:, 0:1], XFM(k),
                   start=(k == 0), stop=(k == KD - 1))
                r = mm(ps[5][0:1, 0:GT], ONES[:, 0:1], SQSC(k),
                       start=(k == 0), stop=(k == KD - 1),
                       inc="TV" if k == KD - 1 else None)
            wait("vector", "TV", r)
            vop(lambda e: e.tensor_scalar_mul(R0[0:1, 0:GT], ps[4][0:1, 0:GT], 1.0 / E))
            vop(lambda e: e.tensor_scalar_mul(R1[0:1, 0:GT], ps[5][0:1, 0:GT], 1.0 / E))
            vop(lambda e: e.tensor_mul(R2[0:1, 0:GT], R0[0:1, 0:GT], R0[0:1, 0:GT]))
            vdrain()
            vop(lambda e: e.tensor_sub(R1[0:1, 0:GT], R1[0:1, 0:GT], R2[0:1, 0:GT]))
            vdrain()
            vst = vop(lambda e: e.tensor_scalar_add(R1[0:1, 0:GT], R1[0:1, 0:GT],
                                                    LN_EPS), inc="VV")
            bank_free[4] = ("VV", vst)
            bank_free[5] = ("VV", vst)
            wait("scalar", "VV", vst)
            ssq = sop(lambda e: e.activation(R1[0:1, 0:GT], R1[0:1, 0:GT], AF.Sqrt),
                      inc="SC")
            wait("vector", "SC", ssq)
            vop(lambda e: e.reciprocal(R1[0:1, 0:GT], R1[0:1, 0:GT]))
            vdrain()
            vst2 = vop(lambda e: e.tensor_mul(R2[0:1, 0:GT], R0[0:1, 0:GT],
                                              R1[0:1, 0:GT]), inc="VV")
            wait("tensor", "VV", vst2)
            for (row, dst) in [(R1, ISTD), (R2, MISTD)]:
                use_bank(6)
                r = mm(ps[6][:, 0:GT], ONES[0:1, :], row[0:1, 0:GT],
                       start=True, stop=True, inc="TV")
                wait("vector", "TV", r)
                ev = vop(lambda e, d=dst: e.tensor_copy(d, ps[6][:, 0:GT]), inc="VV")
                bank_free[6] = ("VV", ev)
            wait("vector", "VV", v_mod)
            h1v = None
            for k in range(KD):
                vop(lambda e, k=k: e.tensor_mul(SQSC(k), XFM(k), ISTD))
            for k in range(KD):
                vop(lambda e, k=k: e.tensor_sub(SQSC(k), SQSC(k), MISTD))
            for k in range(KD):
                vop(lambda e, k=k: e.tensor_scalar(
                    SQSC(k), SQSC(k), MOD[:, 16 + k:17 + k], MOD[:, k:k + 1],
                    ALU.mult, ALU.add))
            for k in range(KD):
                vop(lambda e, k=k: e.tensor_copy(H1R(k), SQSC(k)))
            for k in range(KD):
                h1v = vop(lambda e, k=k: e.tensor_sub(
                    H1LO(k), SQSC(k), H1R(k).bitcast(F32)),
                    inc="VV" if k == KD - 1 else None)
            if tap == "h1" and g == 0:
                wait("sync", "VV", h1v)
                finish(A2[:, 0:2 * KD * GT].bitcast(F32), 2 * KD * GT)
                return nc
            wait("vector", "VV", h1v)
            rg = None
            for (mat_t, dst_r, dst_lo, bank) in [(wq1t, G1QR, G1QLO, 2),
                                                 (wk1t, G1KR, G1KLO, 3)]:
                wv1 = None
                wait2("sync", wvh_free[0], wvh_free[1], ("TV", rg or 0))
                for k in range(KD):
                    wait2("sync", w1tmp_free)
                    dd = dma(W1TMP, mat_t[k], sem="DLX")
                    wait("vector", "DLX", dd)
                    if rg:
                        wait("vector", "TV", rg)
                    vop(lambda e, k=k: e.tensor_copy(W1Rv(k), W1TMP))
                    wv1 = vop(lambda e, k=k: e.tensor_sub(
                        W1LOv(k), W1TMP, W1Rv(k).bitcast(F32)), inc="VV")
                    w1tmp_free = ("VV", wv1)
                wait("tensor", "VV", wv1)
                use_bank(bank)
                for k in range(KD):
                    mm(ps[bank][:, 0:GT], W1Rv(k), H1R(k), start=(k == 0), stop=False)
                    mm(ps[bank][:, 0:GT], W1LOv(k), H1R(k), start=False, stop=False)
                    rg = mm(ps[bank][:, 0:GT], W1Rv(k), H1LO(k), start=False,
                            stop=(k == KD - 1), inc="TV" if k == KD - 1 else None)
                wait("vector", "TV", rg)
                vop(lambda e, d=dst_r, b=bank: e.tensor_copy(d, ps[b][:, 0:GT]))
                vdrain()
                g1d = vop(lambda e, dr=dst_r, dl=dst_lo, b=bank: e.tensor_sub(
                    dl, ps[b][:, 0:GT], dr.bitcast(F32)), inc="VV")
                bank_free[bank] = ("VV", g1d)
            wvh_free = [("TV", rg), ("TV", rg)]
            def emit_wv_dma(cc2):
                p2_ = cc2 % 2
                wait2("sync", wvraw_free[p2_])
                return dma(WVRAW(p2_).rearrange("p (a n) -> p a n", n=128),
                           wvr[cc2].rearrange("a p n -> p a n"),
                           sem="DLV0" if p2_ == 0 else "DLV1")

            def emit_w2_dma(cc2):
                wait2("sync", w2tmp_free[0])
                dma(W2TMP[:, 0:128], w2c_q[cc2], sem="DL2")
                return dma(W2TMP[:, 128:256], w2c_k[cc2], sem="DL2")

            d_wv = emit_wv_dma(0)
            d_w2 = emit_w2_dma(0)
            wait("vector", "DLV0", d_wv)
            wait2("vector", wvh_free[0])
            wait2("vector", wvh_free[1])
            wait("vector", "DL2", d_w2)
            wait2("vector", w2p_free[0])
            vop(lambda e: e.tensor_copy(WVHr(0), WVRAW(0)[:, 0:1024]))
            vop(lambda e: e.tensor_copy(WVHr(1), WVRAW(0)[:, 1024:2048]))
            vop(lambda e: e.tensor_copy(W2Rv(0), W2TMP))
            s0 = vop(lambda e: e.tensor_sub(WVHlo(0), WVRAW(0)[:, 0:1024],
                                            WVHr(0).bitcast(F32)), inc="VV")
            s1 = vop(lambda e: e.tensor_sub(WVHlo(1), WVRAW(0)[:, 1024:2048],
                                            WVHr(1).bitcast(F32)), inc="VV")
            w2v0 = vop(lambda e: e.tensor_sub(W2LOv(0), W2TMP,
                                              W2Rv(0).bitcast(F32)), inc="VV")
            spl_v = [s0, s1]
            w2split_v = w2v0
            wvraw_free[0] = ("VV", s1)
            w2tmp_free[0] = ("VV", w2v0)
            qv = None
            ss_last = None
            v_scan = 0
            for cc in range(KC):
                p = cc % 2
                pn = (cc + 1) % 2
                last_cc = (cc == KC - 1)
                if not last_cc:
                    d_wv = emit_wv_dma(cc + 1)
                    d_w2 = emit_w2_dma(cc + 1)
                use_bank(p)
                rvh = [None, None]
                for hh in range(2):
                    wait("tensor", "VV", spl_v[hh])
                    for kk in range(8):
                        k = hh * 8 + kk
                        first = (hh == 0 and kk == 0)
                        last = (hh == 1 and kk == 7)
                        mm(ps[p][:, 0:GT], WVHr(hh)[:, kk * 128:(kk + 1) * 128],
                           H1R(k), start=first, stop=False)
                        mm(ps[p][:, 0:GT], WVHlo(hh)[:, kk * 128:(kk + 1) * 128],
                           H1R(k), start=False, stop=False)
                        rvh[hh] = mm(ps[p][:, 0:GT],
                                     WVHr(hh)[:, kk * 128:(kk + 1) * 128],
                                     H1LO(k), start=False, stop=last,
                                     inc="TV" if kk == 7 else None)
                wait("tensor", "VV", w2split_v)
                use_bank(2)
                mm(ps[2][:, 0:GT], W2Rv(p)[:, 128:256], G1KR, start=True, stop=False)
                mm(ps[2][:, 0:GT], W2LOv(p)[:, 128:256], G1KR, start=False, stop=False)
                rk = mm(ps[2][:, 0:GT], W2Rv(p)[:, 128:256], G1KLO,
                        start=False, stop=True, inc="TV")
                use_bank(3)
                mm(ps[3][:, 0:n_own], W2Rv(p)[:, 0:128], G1QR[:, own_lo:GT],
                   start=True, stop=False)
                mm(ps[3][:, 0:n_own], W2LOv(p)[:, 0:128], G1QR[:, own_lo:GT],
                   start=False, stop=False)
                rq = mm(ps[3][:, 0:n_own], W2Rv(p)[:, 0:128], G1QLO[:, own_lo:GT],
                        start=False, stop=True, inc="TV")
                w2p_free[p] = ("TV", rq)
                wait("scalar", "TV", rk)
                if v_scan:
                    wait("scalar", "VV", v_scan)
                sop(lambda e: e.activation(FB, ps[2][:, 0:GT], AF.Sigmoid))
                wait("scalar", "TV", rq)
                sqq_prev = qv
                if sqq_prev:
                    wait("scalar", "VV", sqq_prev)
                sop(lambda e, n=n_own: e.activation(
                    SQB[:, 0:n], ps[3][:, 0:n], AF.Sigmoid))
                sa = sop(lambda e: e.activation(AB, FB, AF.Sigmoid), inc="SC")
                wait("vector", "TV", rvh[0])
                wait("vector", "SC", sa)
                vop(lambda e: e.tensor_scalar(KB, AB, -1.0, 1.0, ALU.mult, ALU.add))
                if not last_cc:
                    wait("vector", "DLV0" if pn == 0 else "DLV1", d_wv)
                    vop(lambda e, pn=pn: e.tensor_copy(
                        WVHr(0), WVRAW(pn)[:, 0:1024]))
                else:
                    vdrain()
                wait("vector", "TV", rvh[1])
                vop(lambda e, p=p: e.tensor_mul(BB, KB, ps[p][:, 0:GT]))
                if not last_cc:
                    wait("vector", "TV", rvh[1])
                    vop(lambda e, pn=pn: e.tensor_copy(
                        WVHr(1), WVRAW(pn)[:, 1024:2048]))
                else:
                    vdrain()
                vb = None
                if g == 0:
                    vb = vop(lambda e: e.tensor_mul(BB, BB, MASK), inc="VV")
                if not last_cc:
                    wait("vector", "DL2", d_w2)
                    wait2("vector", w2p_free[pn])
                    vop(lambda e, pn=pn: e.tensor_copy(W2Rv(pn), W2TMP))
                else:
                    vdrain()
                vop(lambda e, cc=cc: e.tensor_tensor_scan(
                    HB, AB, BB, HCAR[:, cc:cc + 1], ALU.mult, ALU.add))
                if vb is None:
                    vb = C["VV"]
                if not last_cc:
                    w0 = vop(lambda e, pn=pn: e.tensor_sub(
                        WVHlo(0), WVRAW(pn)[:, 0:1024], WVHr(0).bitcast(F32)),
                        inc="VV")
                else:
                    vdrain()
                    w0 = None
                v_scan = vop(lambda e, cc=cc: e.tensor_copy(
                    HCAR[:, cc:cc + 1], HB[:, GT - 1:GT]), inc="VV")
                bank_free[p] = ("VV", v_scan)
                sqa = vop(lambda e, n=n_own: e.tensor_mul(
                    SQB[:, 0:n], SQB[:, 0:n], ps[3][:, 0:n]), inc="VV")
                bank_free[3] = ("VV", sqa)
                if not last_cc:
                    w1 = vop(lambda e, pn=pn: e.tensor_sub(
                        WVHlo(1), WVRAW(pn)[:, 1024:2048], WVHr(1).bitcast(F32)),
                        inc="VV")
                    wvraw_free[pn] = ("VV", w1)
                else:
                    vdrain()
                    w1 = None
                vop(lambda e, n=n_own, o=own_lo: e.tensor_mul(
                    SQB[:, 0:n], SQB[:, 0:n], HB[:, o:GT]))
                if not last_cc:
                    w2s = vop(lambda e, pn=pn: e.tensor_sub(
                        W2LOv(pn), W2TMP, W2Rv(pn).bitcast(F32)), inc="VV")
                    w2tmp_free[0] = ("VV", w2s)
                    w2split_v = w2s
                else:
                    vdrain()
                qv = vop(lambda e, cc=cc, n=n_own: e.tensor_scalar(
                    QH(cc, n), SQB[:, 0:n], NW[:, cc:cc + 1], None, ALU.mult),
                    inc="VV")
                if not last_cc:
                    spl_v = [w0, w1]
                wait("scalar", "VV", qv)
                scq = sop(lambda e, n=n_own: e.activation(
                    KB[:, 0:n], SQB[:, 0:n], AF.Square), inc="SC")
                wait("tensor", "SC", scq)
                if cc == 0:
                    use_bank(4)
                ss_last = mm(ps[4][0:1, 0:n_own], ONES[:, 0:1], KB[:, 0:n_own],
                             start=(cc == 0), stop=(cc == KC - 1), inc="TV")
            wvh_free = [("TV", rvh[0]), ("TV", rvh[1])]
            if tap == "scan" and g == 0:
                wait("sync", "VV", qv)
                finish(fv(A1, 0, 32 * n_own), 32 * n_own)
                return nc
            wait("vector", "TV", ss_last)
            vsr = vop(lambda e, n=n_own: e.tensor_scalar(
                R0[0:1, 0:n], ps[4][0:1, 0:n], 1.0 / E2, RMS_EPS, ALU.mult,
                ALU.add), inc="VV")
            bank_free[4] = ("VV", vsr)
            wait("scalar", "VV", vsr)
            ssq2 = sop(lambda e, n=n_own: e.activation(R0[0:1, 0:n], R0[0:1, 0:n],
                                                       AF.Sqrt), inc="SC")
            wait("vector", "SC", ssq2)
            vrr = vop(lambda e, n=n_own: e.reciprocal(R0[0:1, 0:n], R0[0:1, 0:n]),
                      inc="VV")
            wait("tensor", "VV", vrr)
            use_bank(6)
            rb = mm(ps[6][:, 0:n_own], ONES[0:1, :], R0[0:1, 0:n_own],
                    start=True, stop=True, inc="TV")
            wait("vector", "TV", rb)
            erb = vop(lambda e, n=n_own: e.tensor_copy(RSTD[:, 0:n],
                                                       ps[6][:, 0:n]), inc="VV")
            bank_free[6] = ("VV", erb)
            wait("vector", "VV", erb)
            wos_req = [wvraw_free[0], wvraw_free[1], wos_free[0], wos_free[1],
                       prep_state["free"], xtmp_free]
            for mc in range(KD):
                p2 = mc % 2
                if mc < 2:
                    wait2("sync", *wos_req)
                else:
                    wait2("sync", wos_free[p2])
                dwo = dma(WOS(p2).rearrange("p (a n) -> p a n", n=128),
                          wor[mc].rearrange("a p n -> p a n"),
                          sem="DLWA" if p2 == 0 else "DLWB")
                bank = 5 + p2
                wait("tensor", "DLWA" if p2 == 0 else "DLWB", dwo)
                use_bank(bank)
                rwo = None
                for cc in range(KC):
                    rwo = mm(ps[bank][:, 0:n_own],
                             WOS(p2)[:, cc * 128:(cc + 1) * 128], QH(cc, n_own),
                             start=(cc == 0), stop=(cc == KC - 1),
                             inc="TV" if cc == KC - 1 else None)
                wos_free[p2] = ("TV", rwo)
                wait("vector", "TV", rwo)
                wait2("vector", x1t_free[p2])
                vop(lambda e, b=bank, n=n_own, p2=p2: e.tensor_mul(
                    X1T(p2)[:, 0:n], ps[b][:, 0:n], RSTD[:, 0:n]))
                vdrain()
                vx1 = vop(lambda e, mc=mc, n=n_own, o=own_lo, p2=p2:
                          e.scalar_tensor_tensor(
                              X1T(p2)[:, 0:n], X1T(p2)[:, 0:n],
                              MOD[:, 32 + mc:33 + mc], XFM(mc)[:, o:GT],
                              ALU.mult, ALU.add), inc="VV")
                bank_free[bank] = ("VV", vx1)
                wait("sync", "VV", vx1)
                stx = dma(x1s[:, mc * OWN + own_glob:mc * OWN + own_glob + n_own],
                          X1T(p2)[:, 0:n_own], sem="STA" if p2 == 0 else "STB")
                x1t_free[p2] = ("STA" if p2 == 0 else "STB", stx)
            wvraw_free = [("TV", rwo), ("TV", rwo)]
            prep_state["free"] = ("TV", rwo)
            emit_prepass(64)
        scales_done = emit_scales()
        tv_attn = C["TV"]
        vv_attn = C["VV"]
        if tap == "x1":
            wait("sync", "VV", scales_done)
            wait("sync", "STA", C["STA"])
            wait("sync", "STB", C["STB"])
            dwb = dma(A4[:, 0:8192], x1s[:, 0:8192])
            wait("sync", "DLX", dwb)
            finish(A4[:, 0:8192], 8192)
            return nc

        x2_free = [("TV", tv_attn), ("VV", vv_attn),
                   ("STA", C["STA"]), ("STB", C["STB"])]
        wfl_free = ("TV", tv_attn)
        hid_free = [("TV", tv_attn)] * 4
        tds_free = [("TV", tv_attn)] * 4
        tug_free = ("TV", tv_attn)
        gts_free = ("TV", tv_attn)
        dacc_v = [0] * KD
        rdn = None
        for h in range(NH):
            if isinstance(x2_free, list):
                wait2("sync", *x2_free)
            else:
                wait2("sync", x2_free)
            dx = None
            for mc in range(KD):
                dx = dma(X2H(mc), x1s[:, mc * OWN + h * HT:mc * OWN + (h + 1) * HT])
            wait("scalar", "DLX", dx)
            if h == 0:
                wait("scalar", "TV", tv_attn)
            sq_done = None
            for k in range(KD):
                sq_done = sop(lambda e, k=k: e.activation(
                    H2SQ(k), X2H(k), AF.Square), inc="SC" if k == KD - 1 else None)
            wait("tensor", "SC", sq_done)
            use_bank(4)
            use_bank(5)
            r = None
            for k in range(KD):
                mm(ps[4][0:1, 0:HT], ONES[:, 0:1], X2H(k),
                   start=(k == 0), stop=(k == KD - 1))
                r = mm(ps[5][0:1, 0:HT], ONES[:, 0:1], H2SQ(k),
                       start=(k == 0), stop=(k == KD - 1),
                       inc="TV" if k == KD - 1 else None)
            wait("vector", "TV", r)
            vop(lambda e: e.tensor_scalar_mul(R0[0:1, 0:HT], ps[4][0:1, 0:HT], 1.0 / E))
            vop(lambda e: e.tensor_scalar_mul(R1[0:1, 0:HT], ps[5][0:1, 0:HT], 1.0 / E))
            vop(lambda e: e.tensor_mul(R2[0:1, 0:HT], R0[0:1, 0:HT], R0[0:1, 0:HT]))
            vdrain()
            vop(lambda e: e.tensor_sub(R1[0:1, 0:HT], R1[0:1, 0:HT], R2[0:1, 0:HT]))
            vdrain()
            vst = vop(lambda e: e.tensor_scalar_add(R1[0:1, 0:HT], R1[0:1, 0:HT],
                                                    LN_EPS), inc="VV")
            bank_free[4] = ("VV", vst)
            bank_free[5] = ("VV", vst)
            wait("scalar", "VV", vst)
            ssq = sop(lambda e: e.activation(R1[0:1, 0:HT], R1[0:1, 0:HT], AF.Sqrt),
                      inc="SC")
            wait("vector", "SC", ssq)
            vop(lambda e: e.reciprocal(R1[0:1, 0:HT], R1[0:1, 0:HT]))
            vdrain()
            vst2 = vop(lambda e: e.tensor_mul(R2[0:1, 0:HT], R0[0:1, 0:HT],
                                              R1[0:1, 0:HT]), inc="VV")
            wait("tensor", "VV", vst2)
            for (row, dst) in [(R1, I2), (R2, M2)]:
                use_bank(6)
                r = mm(ps[6][:, 0:HT], ONES[0:1, :], row[0:1, 0:HT],
                       start=True, stop=True, inc="TV")
                wait("vector", "TV", r)
                ev = vop(lambda e, d=dst: e.tensor_copy(d, ps[6][:, 0:HT]), inc="VV")
                bank_free[6] = ("VV", ev)
            wait("vector", "VV", scales_done)
            h2v = None
            for k in range(KD):
                vop(lambda e, k=k: e.tensor_mul(H2SQ(k), X2H(k), I2))
            for k in range(KD):
                vop(lambda e, k=k: e.tensor_sub(H2SQ(k), H2SQ(k), M2))
            for k in range(KD):
                vop(lambda e, k=k: e.tensor_scalar(
                    H2SQ(k), H2SQ(k), MOD[:, 64 + k:65 + k], MOD[:, 48 + k:49 + k],
                    ALU.mult, ALU.add))
            for k in range(KD):
                vop(lambda e, k=k: e.tensor_copy(H2R(k), H2SQ(k)))
            for k in range(KD):
                h2v = vop(lambda e, k=k: e.tensor_sub(
                    H2LO(k), H2SQ(k), H2R(k).bitcast(F32)),
                    inc="VV" if k == KD - 1 else None)
            if tap == "h2" and h == 0:
                wait("sync", "VV", h2v)
                finish(A1[:, 0:8192].bitcast(F32), 8192)
                return nc
            wait2("vector", wos_free[0], wos_free[1], prep_state["free"],
                  wvraw_free[0], wvraw_free[1])
            dz = vop(lambda e: e.memset(A4[:, 0:8192], 0.0), inc="VV")
            for mc in range(KD):
                dacc_v[mc] = dz
            for eg in range(KE // 4):
                hv = None
                for j in range(4):
                    ec = eg * 4 + j
                    tv_list = []
                    for (mat, dst_t, scix) in [(wupr, 0, 3), (wgater, 1, 4),
                                               (wdownr, 2, 5)]:
                        wait2("sync", wfl_free)
                        dd = dma(WFL.rearrange("p (a n) -> p a n", n=128),
                                 mat[ec].rearrange("a p n -> p a n"), sem="DLF")
                        wait("vector", "DLF", dd)
                        if dst_t == 2:
                            wait2("vector", tds_free[j])
                        else:
                            wait2("vector", tug_free)
                        wait2("vector", gts_free)
                        dap = (TUG(dst_t) if dst_t < 2 else TDS(j))
                        tv = None
                        for pair in range(2):
                            c0, c1 = pair * 2, pair * 2 + 1
                            for cch in (c0, c1):
                                g0_ = (cch % 2) * 512
                                vop(lambda e, cch=cch, g0_=g0_, scix=scix:
                                    e.tensor_scalar(
                                        GTSC[:, g0_:g0_ + 512],
                                        WFL[:, cch * 512:(cch + 1) * 512],
                                        SCOL[:, scix:scix + 1], None, ALU.is_gt))
                                vop(lambda e, cch=cch, g0_=g0_, scix=scix:
                                    e.tensor_scalar(
                                        LTSC[:, g0_:g0_ + 512],
                                        WFL[:, cch * 512:(cch + 1) * 512],
                                        SCOL[:, scix + 3:scix + 4], None,
                                        ALU.is_lt))
                            for cch in (c0, c1):
                                g0_ = (cch % 2) * 512
                                tv = vop(lambda e, cch=cch, g0_=g0_, dap=dap:
                                         e.tensor_sub(
                                             dap[:, cch * 512:(cch + 1) * 512],
                                             GTSC[:, g0_:g0_ + 512],
                                             LTSC[:, g0_:g0_ + 512]),
                                         inc="VV" if cch == c1 else None)
                        wfl_free = ("VV", tv)
                        gts_free = ("VV", tv)
                        tv_list.append(tv)
                    wait("tensor", "VV", tv_list[1])
                    wait("tensor", "VV", h2v)
                    use_bank(0)
                    use_bank(1)
                    ru = None
                    for k in range(KD):
                        mm(ps[0][:, 0:HT], TUG(0)[:, k * 128:(k + 1) * 128], H2R(k),
                           start=(k == 0), stop=False)
                        mm(ps[0][:, 0:HT], TUG(0)[:, k * 128:(k + 1) * 128], H2LO(k),
                           start=False, stop=(k == KD - 1))
                        mm(ps[1][:, 0:HT], TUG(1)[:, k * 128:(k + 1) * 128], H2R(k),
                           start=(k == 0), stop=False)
                        ru = mm(ps[1][:, 0:HT], TUG(1)[:, k * 128:(k + 1) * 128],
                                H2LO(k), start=False, stop=(k == KD - 1),
                                inc="TV" if k == KD - 1 else None)
                    tug_free = ("TV", ru)
                    wait("scalar", "TV", ru)
                    sg = sop(lambda e: e.activation(SGB, ps[0][:, 0:HT], AF.Sigmoid,
                                                    scale=SCOL[:, 0:1]), inc="SC")
                    wait("vector", "SC", sg)
                    wait2("vector", hid_free[j])
                    vop(lambda e: e.tensor_mul(T1B, SGB, ps[0][:, 0:HT]))
                    vdrain()
                    vop(lambda e: e.tensor_mul(HPL, T1B, ps[1][:, 0:HT]))
                    vdrain()
                    vop(lambda e, j=j: e.tensor_copy(HIDR(j), HPL))
                    vdrain()
                    hv = vop(lambda e, j=j: e.tensor_sub(
                        HIDLO(j), HPL, HIDR(j).bitcast(F32)), inc="VV")
                    bank_free[0] = ("VV", hv)
                    bank_free[1] = ("VV", hv)
                wait("tensor", "VV", hv)
                for mc in range(KD):
                    bank = 2 + mc % 2
                    use_bank(bank)
                    for j in range(4):
                        mm(ps[bank][:, 0:HT], TDS(j)[:, mc * 128:(mc + 1) * 128],
                           HIDR(j), start=(j == 0), stop=False)
                        rdn = mm(ps[bank][:, 0:HT],
                                 TDS(j)[:, mc * 128:(mc + 1) * 128], HIDLO(j),
                                 start=False, stop=(j == 3),
                                 inc="TV" if j == 3 else None)
                    wait("vector", "TV", rdn)
                    wait("vector", "VV", dacc_v[mc])
                    dacc_v[mc] = vop(lambda e, mc=mc, b=bank: e.tensor_add(
                        DACC(mc), DACC(mc), ps[b][:, 0:HT]), inc="VV")
                    bank_free[bank] = ("VV", dacc_v[mc])
                for j in range(4):
                    hid_free[j] = ("TV", rdn)
                    tds_free[j] = ("TV", rdn)
            wait("sync", "TV", rdn)
            dx2 = None
            for mc in range(KD):
                dx2 = dma(X2H(mc), x1s[:, mc * OWN + h * HT:mc * OWN + (h + 1) * HT])
            wait("vector", "DLX", dx2)
            xv = None
            for mc in range(KD):
                wait("vector", "VV", dacc_v[mc])
                xv = vop(lambda e, mc=mc: e.scalar_tensor_tensor(
                    X2H(mc), DACC(mc), GBS[:, mc:mc + 1], X2H(mc),
                    ALU.mult, ALU.add), inc="VV")
            wait("tensor", "VV", xv)
            xo_free = ("VV", 0)
            for tt in range(HT // 128):
                evo = None
                for mc in range(KD):
                    bank = 6 + mc % 2
                    use_bank(bank)
                    rto = tr(ps[bank][:, 0:128],
                             X2H(mc)[:, tt * 128:(tt + 1) * 128], inc="TV")
                    wait("vector", "TV", rto)
                    wait2("vector", xo_free)
                    evo = vop(lambda e, mc=mc, b=bank: e.tensor_copy(
                        XO[:, mc * 128:(mc + 1) * 128], ps[b][:, 0:128]), inc="VV")
                    bank_free[bank] = ("VV", evo)
                wait("sync", "VV", evo)
                row0 = h * HT + tt * 128
                dma(xout[row0:row0 + 128, :], XO, sem="STO")
                xo_free = ("STO", C["STO"])
            x2_free = [("STO", C["STO"]), ("STA", C["STA"]), ("STB", C["STB"])]
            wfl_free = ("STO", C["STO"])

        finish()
        return nc


def emit_all(block, STR):
    @block.sync
    def _(sync):
        for f in STR["sync"]:
            f(sync)

    @block.tensor
    def _(tensor):
        for f in STR["tensor"]:
            f(tensor)

    @block.vector
    def _(vector):
        for f in STR["vector"]:
            f(vector)

    @block.scalar
    def _(scalar):
        for f in STR["scalar"]:
            f(scalar)


_NC_CACHE = {}


def _get_nc(tap=None):
    if tap not in _NC_CACHE:
        _NC_CACHE[tap] = build_nc(tap)
    return _NC_CACHE[tap]


def stage_inputs(inputs):
    f32 = np.float32
    x = np.ascontiguousarray(np.asarray(inputs["L0_attn_x"], dtype=f32))
    c = np.ascontiguousarray(np.asarray(inputs["L0_c"], dtype=f32))

    wv_r = np.ascontiguousarray(np.asarray(
        inputs["L0_attn_Wv"], dtype=f32).reshape(KD, 128, KC, 128).transpose(2, 0, 1, 3))
    wo_r = np.ascontiguousarray(np.asarray(
        inputs["L0_attn_Wo"], dtype=f32).reshape(KC, 128, KD, 128).transpose(2, 0, 1, 3))
    wup_r = np.ascontiguousarray(np.asarray(
        inputs["L0_ffn_W_up"], dtype=f32).T.reshape(KD, 128, KE, 128).transpose(2, 0, 1, 3))
    wgate_r = np.ascontiguousarray(np.asarray(
        inputs["L0_ffn_W_gate"], dtype=f32).T.reshape(KD, 128, KE, 128).transpose(2, 0, 1, 3))
    wdown_r = np.ascontiguousarray(np.asarray(
        inputs["L0_ffn_W_down"], dtype=f32).T.reshape(KE, 128, KD, 128).transpose(0, 2, 1, 3))
    wmod_r = np.ascontiguousarray(np.asarray(
        inputs["L0_wmod"], dtype=f32).reshape(KD, 128, 24, 512).transpose(2, 0, 1, 3))
    normw_r = np.ascontiguousarray(np.asarray(
        inputs["L0_attn_norm_w"], dtype=f32).reshape(KC, 128).T)
    idt = np.eye(128, dtype=f32)
    wq1 = np.ascontiguousarray(np.asarray(inputs["L0_attn_Wq1"], dtype=f32))
    wk1 = np.ascontiguousarray(np.asarray(inputs["L0_attn_Wk1"], dtype=f32))
    wq2 = np.ascontiguousarray(np.asarray(inputs["L0_attn_Wq2"], dtype=f32))
    wk2 = np.ascontiguousarray(np.asarray(inputs["L0_attn_Wk2"], dtype=f32))

    in_maps = []
    for core in range(8):
        b, q = core // 4, core % 4
        if q == 0:
            xw = np.concatenate([x[b, 0:HALO], x[b, 0:OWN]], axis=0)
            mk = np.ones((1, GT), dtype=f32)
            mk[0, 0:HALO] = 0.0
        else:
            xw = x[b, q * OWN - HALO:(q + 1) * OWN]
            mk = np.ones((1, GT), dtype=f32)
        mask = np.ascontiguousarray(np.broadcast_to(mk, (128, GT)))
        cre = np.ascontiguousarray(c[b].reshape(KD, 128).T)
        in_maps.append({
            "xwin": np.ascontiguousarray(xw), "maskr": mask, "cresh": cre,
            "wq1": wq1, "wk1": wk1, "wq2": wq2, "wk2": wk2,
            "wvr": wv_r, "wor": wo_r, "wupr": wup_r, "wgater": wgate_r,
            "wdownr": wdown_r, "wmodr": wmod_r, "normwr": normw_r,
            "idt_in": idt,
        })
    return in_maps


def kernel(**inputs):
    in_maps = stage_inputs(inputs)
    nc = _get_nc()
    res = run_bass_kernel_spmd(nc, in_maps, core_ids=list(range(8)))
    out = np.empty((2, 4096, E), dtype=np.float32)
    for core in range(8):
        b, q = core // 4, core % 4
        out[b, q * OWN:(q + 1) * OWN, :] = res.results[core]["xout"]
    return out


# version 05
# speedup vs baseline: 1.0697x; 1.0697x over previous
import numpy as np
from contextlib import ExitStack

import concourse.bass as bass
import concourse.mybir as mybir
import concourse.bass_utils as _bass_utils
from concourse.bass_utils import run_bass_kernel_spmd

_orig_run_command = _bass_utils.run_command


def _run_command_no_birverifier(argv, **kwargs):
    argv = [a.replace("birverifier,", "") if isinstance(a, str) else a
            for a in argv]
    return _orig_run_command(argv, **kwargs)


_bass_utils.run_command = _run_command_no_birverifier

F32 = mybir.dt.float32
F32R = mybir.dt.float32r
AF = mybir.ActivationFunctionType
ALU = mybir.AluOpType

E = 2048
E2 = 4096
DF = 8192
HD = 128
OWN = 1024
HALO = 128
T = OWN + HALO
KD = E // 128
KC = E2 // 128
KE = DF // 128
NG = 3
GT = T // NG
NH = 2
HT = OWN // NH
LN_EPS = 1e-5
RMS_EPS = 1e-6
Q_EPS = 1e-8


def build_nc(tap=None, serial_dma=False, race_detect=True):
    nc = bass.Bass("TRN2", target_bir_lowering=False,
                   detect_race_conditions=race_detect)

    xwin = nc.dram_tensor("xwin", [T, E], F32, kind="ExternalInput").ap()
    maskr = nc.dram_tensor("maskr", [128, GT], F32, kind="ExternalInput").ap()
    cresh = nc.dram_tensor("cresh", [128, KD], F32, kind="ExternalInput").ap()
    wq1 = nc.dram_tensor("wq1", [E, HD], F32, kind="ExternalInput").ap()
    wk1 = nc.dram_tensor("wk1", [E, HD], F32, kind="ExternalInput").ap()
    wq2 = nc.dram_tensor("wq2", [HD, E2], F32, kind="ExternalInput").ap()
    wk2 = nc.dram_tensor("wk2", [HD, E2], F32, kind="ExternalInput").ap()
    wvr = nc.dram_tensor("wvr", [KC, KD, 128, 128], F32, kind="ExternalInput").ap()
    wor = nc.dram_tensor("wor", [KD, KC, 128, 128], F32, kind="ExternalInput").ap()
    wupr = nc.dram_tensor("wupr", [KE, KD, 128, 128], F32, kind="ExternalInput").ap()
    wgater = nc.dram_tensor("wgater", [KE, KD, 128, 128], F32, kind="ExternalInput").ap()
    wdownr = nc.dram_tensor("wdownr", [KE, KD, 128, 128], F32, kind="ExternalInput").ap()
    wmodr = nc.dram_tensor("wmodr", [24, KD, 128, 512], F32, kind="ExternalInput").ap()
    normwr = nc.dram_tensor("normwr", [128, KC], F32, kind="ExternalInput").ap()
    idt_in = nc.dram_tensor("idt_in", [128, 128], F32, kind="ExternalInput").ap()
    xout = nc.dram_tensor("xout", [OWN, E], F32, kind="ExternalOutput").ap()
    x1s = nc.dram_tensor("x1s", [128, KD * OWN], F32).ap()

    with ExitStack() as ctx:
        en = ctx.enter_context

        A1 = en(nc.sbuf_tensor("A1", [128, 12288], F32R))
        A2 = en(nc.sbuf_tensor("A2", [128, 12288], F32R))
        A3 = en(nc.sbuf_tensor("A3", [128, 8192], F32R))
        A4 = en(nc.sbuf_tensor("A4", [128, 8192], F32))
        A5 = en(nc.sbuf_tensor("A5", [128, 5120], F32R))
        A6 = en(nc.sbuf_tensor("A6", [128, 4608], F32R))
        A7 = en(nc.sbuf_tensor("A7", [128, 1536], F32))
        MKC = en(nc.sbuf_tensor("MKC", [128, 640], F32))

        def fv(t, a, b):
            return t[:, a:b].bitcast(F32)

        def QH(cc, n):
            return fv(A1, cc * n, (cc + 1) * n)

        def SQSC(k):
            return fv(A1, k * GT, (k + 1) * GT)

        def H1R(k):
            return A2[:, k * GT:(k + 1) * GT]

        def H1LO(k):
            return A2[:, KD * GT + k * GT:KD * GT + (k + 1) * GT]

        def XFM(k):
            return fv(A3, k * GT, (k + 1) * GT)
        RSTD = fv(A3, 6144, 6144 + GT)

        def X1T(par):
            return fv(A3, 6528 + par * GT, 6528 + (par + 1) * GT)

        def WVRAW(p):
            return A4[:, p * 2048:(p + 1) * 2048]
        XTMP = A4[:, 4096:6144]

        def WOS(p):
            return A4[:, p * 4096:(p + 1) * 4096]
        PREP = A4[:, 6144:8192]

        def WVHr(hh):
            return A5[:, hh * 2048:hh * 2048 + 1024]

        def WVHlo(hh):
            return A5[:, hh * 2048 + 1024:(hh + 1) * 2048]

        def W2Rv(p):
            return A5[:, 4096 + p * 512:4096 + p * 512 + 256]

        def W2LOv(p):
            return A5[:, 4096 + p * 512 + 256:4096 + (p + 1) * 512]

        def W1Rv(k):
            return A5[:, k * 128:(k + 1) * 128]

        def W1LOv(k):
            return A5[:, 2048 + k * 128:2048 + (k + 1) * 128]

        G1QR = A6[:, 0:GT]
        G1QLO = A6[:, GT:2 * GT]
        G1KR = A6[:, 2 * GT:3 * GT]
        G1KLO = A6[:, 3 * GT:4 * GT]
        FB = fv(A6, 4 * GT, 5 * GT)
        AB = fv(A6, 5 * GT, 6 * GT)
        KB = fv(A6, 6 * GT, 7 * GT)
        BB = fv(A6, 7 * GT, 8 * GT)
        HB = fv(A6, 8 * GT, 9 * GT)
        SQB = fv(A6, 9 * GT, 10 * GT)
        ISTD = FB
        MISTD = AB
        W1TMP = fv(A6, 3840, 3968)
        W2TMP = fv(A6, 3968, 4224)
        CRE = fv(A6, 4224, 4240)
        SC_ = fv(A6, 4240, 4256)
        MOD = fv(A6, 4256, 4352)
        NW = fv(A6, 4352, 4384)
        HCAR = fv(A6, 4384, 4416)
        SCOL = fv(A6, 4416, 4432)
        ABSC = fv(A6, 4432, 4444)
        GBS = fv(A6, 4444, 4460)
        ROWT = fv(A6, 4460, 4464)
        TT1 = fv(A6, 4464, 4480)
        TT2 = fv(A6, 4480, 4496)
        TT3 = fv(A6, 4496, 4512)
        TT4 = fv(A6, 4512, 4528)
        TT5 = fv(A6, 4528, 4544)
        TT6 = fv(A6, 4544, 4560)
        R0 = A7[0:1, 0:512]
        R1 = A7[0:1, 512:1024]
        R2 = A7[0:1, 1024:1536]
        MASK = fv(MKC, 0, GT)
        IDT = MKC[:, 384:512]
        ONES = MKC[:, 512:640]
        def H2R(k):
            return A2[:, k * HT:(k + 1) * HT]

        def H2SQ(k):
            return fv(A1, k * HT, (k + 1) * HT)

        def H2LO(k):
            return A1[:, k * HT:(k + 1) * HT]

        def X2H(k):
            return fv(A3, k * HT, (k + 1) * HT)

        def HIDR(j):
            return A3[:, j * 1024:j * 1024 + 512]

        def HIDLO(j):
            return A3[:, j * 1024 + 512:(j + 1) * 1024]

        def TUG(m):
            return A3[:, 4096 + m * 2048:4096 + (m + 1) * 2048]

        def DACC(mc):
            return A4[:, mc * HT:(mc + 1) * HT]

        def TDS(j):
            return [A5[:, 0:2048], A5[:, 2048:4096],
                    A1[:, 9216:11264], A2[:, 9216:11264]][j]
        GTSC = fv(A1, 8192, 9216)
        LTSC = fv(A2, 8192, 9216)
        H2T1 = fv(A2, 11264, 11776)
        WFL = fv(A6, 0, 2048)
        SGB = fv(A6, 2048, 2560)
        T1B = fv(A6, 2560, 3072)
        HPL = fv(A6, 3072, 3584)
        XO = fv(A6, 0, 2048)
        I2 = A4[:, 0:512]
        M2 = A4[:, 512:1024]

        ps = [en(nc.psum_tensor(f"ps{i}", [128, 512], F32)) for i in range(8)]

        hs = {}
        for s in ["DLX", "DLV0", "DLV1", "DL2", "DLWA", "DLWB", "DLP", "DLF",
                  "TV", "VV", "SC", "STA", "STB", "STO"]:
            hs[s] = en(nc.semaphore(s))
        C = {k: 0 for k in hs}

        block = en(nc.Block())
        STR = {"sync": [], "tensor": [], "vector": [], "scalar": []}

        def wait(stream, sem, val):
            if val and val > 0:
                STR[stream].append(lambda e, h=hs[sem], v=val: e.wait_ge(h, v))

        def wait2(stream, *svs):
            best = {}
            for sv in svs:
                if sv is None:
                    continue
                s, v = sv
                if v and v > 0:
                    best[s] = max(best.get(s, 0), v)
            for s, v in best.items():
                wait(stream, s, v)

        def dma(dst, src, sem="DLX"):
            C[sem] += 16
            STR["sync"].append(lambda e, d=dst, s=src, h=hs[sem]:
                               e.dma_start(out=d, in_=s).then_inc(h, 16))
            if serial_dma:
                wait("sync", sem, C[sem])
            return C[sem]

        def mm(psap, lhsT, rhs, start, stop, inc=None):
            if inc is None:
                STR["tensor"].append(lambda e, o=psap, a=lhsT, b=rhs, s1=start,
                                     s2=stop: e.matmul(o, a, b, start=s1, stop=s2))
                return None
            C[inc] += 1
            STR["tensor"].append(lambda e, o=psap, a=lhsT, b=rhs, s1=start, s2=stop,
                                 h=hs[inc]: e.matmul(o, a, b, start=s1,
                                                     stop=s2).then_inc(h, 1))
            return C[inc]

        def tr(psap, inap, inc=None):
            if inc is None:
                STR["tensor"].append(lambda e, o=psap, a=inap: e.transpose(o, a, IDT))
                return None
            C[inc] += 1
            STR["tensor"].append(lambda e, o=psap, a=inap, h=hs[inc]:
                                 e.transpose(o, a, IDT).then_inc(h, 1))
            return C[inc]

        def vop(fn, inc=None):
            if inc is None:
                STR["vector"].append(fn)
                return None
            C[inc] += 1
            STR["vector"].append(lambda e, f=fn, h=hs[inc]: f(e).then_inc(h, 1))
            return C[inc]

        def sop(fn, inc=None):
            if inc is None:
                STR["scalar"].append(fn)
                return None
            C[inc] += 1
            STR["scalar"].append(lambda e, f=fn, h=hs[inc]: f(e).then_inc(h, 1))
            return C[inc]

        def vdrain():
            STR["vector"].append(lambda e: e.drain())

        bank_free = [("VV", 0)] * 8

        def use_bank(i):
            wait2("tensor", bank_free[i])

        def finish(extra_src=None, extra_n=0):
            if extra_src is not None:
                xv = xout.rearrange("(p a) d -> p (a d)", p=128)
                dma(xv[:, 0:extra_n], extra_src, sem="STO")
            for st_ in ("STA", "STB", "STO"):
                wait("sync", st_, C[st_])
            emit_all(block, STR)

        dma(IDT, idt_in)
        dma(CRE, cresh)
        dma(NW, normwr)
        d0 = dma(MASK, maskr)
        wait("vector", "DLX", d0)
        vop(lambda e: e.memset(ONES, 1.0))
        vop(lambda e: e.memset(HCAR, 0.0))
        v_ones = vop(lambda e: e.memset(ABSC, 0.0), inc="VV")
        wait("scalar", "DLX", d0)
        wait("scalar", "VV", v_ones)
        c_sg = sop(lambda e: e.activation(SC_, CRE, AF.Sigmoid), inc="SC")
        wait("vector", "SC", c_sg)
        v_sc = vop(lambda e: e.tensor_mul(SC_, SC_, CRE), inc="VV")

        wait("tensor", "VV", v_sc)
        wait("tensor", "DLX", d0)
        mod_free = [("TV", 0), ("TV", 0)]
        r = None
        for jt in range(24):
            for half in range(2):
                p = half
                wait2("sync", mod_free[p])
                dd = dma(WOS(p).rearrange("p (a n) -> p a n", n=512),
                         wmodr[jt, half * 8:(half + 1) * 8].rearrange("a p n -> p a n"),
                         sem="DLWA" if p == 0 else "DLWB")
                wait("tensor", "DLWA" if p == 0 else "DLWB", dd)
                if half == 0:
                    use_bank(6)
                for kk in range(8):
                    k = half * 8 + kk
                    r = mm(ps[6][0:1, 0:512], SC_[:, k:k + 1],
                           WOS(p)[:, kk * 512:(kk + 1) * 512],
                           start=(k == 0), stop=(k == KD - 1),
                           inc="TV" if kk == 7 else None)
                mod_free[p] = ("TV", r)
            wait("vector", "TV", r)
            ev = vop(lambda e: e.tensor_copy(R0, ps[6][0:1, 0:512]), inc="VV")
            bank_free[6] = ("VV", ev)
            wait("tensor", "VV", ev)
            for u in range(4):
                j = jt * 4 + u
                use_bank(7)
                r2 = mm(ps[7][:, 0:1], R0[0:1, u * 128:(u + 1) * 128],
                        ONES[0:1, 0:1], start=True, stop=True, inc="TV")
                wait("vector", "TV", r2)
                ev2 = vop(lambda e, j=j: e.tensor_copy(MOD[:, j:j + 1],
                                                       ps[7][:, 0:1]), inc="VV")
                bank_free[7] = ("VV", ev2)
        vop(lambda e: e.tensor_scalar_add(MOD[:, 16:32], MOD[:, 16:32], 1.0))
        vop(lambda e: e.tensor_scalar_add(MOD[:, 64:80], MOD[:, 64:80], 1.0))

        ga_ = MOD[:, 32:48]
        gb_ = MOD[:, 80:96]
        vop(lambda e: e.tensor_scalar_max(TT1, ga_, 0.25))
        vop(lambda e: e.tensor_scalar_max(TT4, gb_, 0.25))
        vop(lambda e: e.tensor_scalar_min(TT2, ga_, -0.25))
        vop(lambda e: e.tensor_scalar_min(TT5, gb_, -0.25))
        vop(lambda e: e.tensor_single_scalar(TT3, ga_, 0.0, ALU.is_ge))
        vop(lambda e: e.tensor_single_scalar(TT6, gb_, 0.0, ALU.is_ge))
        vop(lambda e: e.tensor_sub(TT1, TT1, TT2))
        vop(lambda e: e.tensor_sub(TT4, TT4, TT5))
        vop(lambda e: e.tensor_mul(TT1, TT1, TT3))
        vop(lambda e: e.tensor_mul(TT4, TT4, TT6))
        vop(lambda e: e.tensor_add(ga_, TT1, TT2))
        vop(lambda e: e.tensor_add(gb_, TT4, TT5))
        v_mod = vop(lambda e: e.engine_nop(), inc="VV")
        if tap == "mod":
            wait("sync", "VV", v_mod)
            finish(MOD, 96)
            return nc

        prep_state = {"i": 0, "free": [mod_free[1]], "vv": 0, "pend": None}
        prep_mats = [wupr, wgater, wdownr]

        def emit_prep_consume():
            if prep_state["pend"] is None:
                return
            dd, mi = prep_state["pend"]
            prep_state["pend"] = None
            wait("scalar", "DLF", dd)
            if prep_state["vv"]:
                wait("scalar", "VV", prep_state["vv"])
            wait2("scalar", bank_free[7])
            sr = None
            for u in range(4):
                sr = sop(lambda e, u=u: e.activation(
                    ps[7][:, 0:512], PREP[:, u * 512:(u + 1) * 512], AF.Abs,
                    accum_out=ROWT[:, u:u + 1]),
                    inc="SC" if u == 3 else None)
            wait("vector", "SC", sr)
            av = None
            for u in range(4):
                av = vop(lambda e, mi=mi, u=u: e.tensor_add(
                    ABSC[:, mi * 4 + u:mi * 4 + u + 1],
                    ABSC[:, mi * 4 + u:mi * 4 + u + 1], ROWT[:, u:u + 1]),
                    inc="VV" if u == 3 else None)
            prep_state["free"] = [("SC", sr)]
            prep_state["vv"] = av
            bank_free[7] = ("SC", sr)

        def emit_prepass(n):
            for _ in range(n):
                emit_prep_consume()
                i = prep_state["i"]
                if i >= 3 * KE:
                    return
                prep_state["i"] += 1
                mi, ec = i // KE, i % KE
                wait2("sync", *prep_state["free"])
                dd = dma(PREP.rearrange("p (a n) -> p a n", n=128),
                         prep_mats[mi][ec].rearrange("a p n -> p a n"), sem="DLF")
                prep_state["pend"] = (dd, mi)

        def emit_scales():
            wait("tensor", "VV", prep_state["vv"])
            for m_ in range(3):
                use_bank(7)
                r = mm(ps[7][0:1, 0:4], ONES[:, 0:1], ABSC[:, m_ * 4:(m_ + 1) * 4],
                       start=True, stop=True, inc="TV")
                wait("vector", "TV", r)
                ev = vop(lambda e, m_=m_: e.tensor_copy(R2[0:1, m_ * 4:m_ * 4 + 4],
                                                        ps[7][0:1, 0:4]), inc="VV")
                bank_free[7] = ("VV", ev)
            vdrain()
            for m_ in range(3):
                vop(lambda e, m_=m_: e.tensor_reduce(
                    R1[0:1, m_:m_ + 1], R2[0:1, m_ * 4:m_ * 4 + 4],
                    mybir.AxisListType.X, ALU.add))
            vdrain()
            vop(lambda e: e.tensor_scalar(R1[0:1, 0:3], R1[0:1, 0:3],
                                          1.0 / (DF * E), Q_EPS, ALU.mult, ALU.add))
            vdrain()
            vop(lambda e: e.tensor_scalar_mul(R1[0:1, 3:6], R1[0:1, 0:3], 0.5))
            vdrain()
            vop(lambda e: e.tensor_scalar_mul(R1[0:1, 6:9], R1[0:1, 3:6], -1.0))
            vop(lambda e: e.tensor_mul(R1[0:1, 9:10], R1[0:1, 0:1], R1[0:1, 1:2]))
            vdrain()
            vr = vop(lambda e: e.tensor_mul(R1[0:1, 9:10], R1[0:1, 9:10],
                                            R1[0:1, 2:3]), inc="VV")
            wait("tensor", "VV", vr)
            use_bank(7)
            r = mm(ps[7][:, 0:10], ONES[0:1, :], R1[0:1, 0:10],
                   start=True, stop=True, inc="TV")
            wait("vector", "TV", r)
            vop(lambda e: e.tensor_copy(SCOL[:, 0:10], ps[7][:, 0:10]))
            vdrain()
            gv = vop(lambda e: e.tensor_scalar(GBS, MOD[:, 80:96], SCOL[:, 9:10],
                                               None, ALU.mult), inc="VV")
            bank_free[7] = ("VV", gv)
            return gv

        xw_t = xwin.rearrange("(a p) d -> a p d", p=128)
        w2c_q = wq2.rearrange("p (c n) -> c p n", n=128)
        w2c_k = wk2.rearrange("p (c n) -> c p n", n=128)
        wq1t = wq1.rearrange("(k p) m -> k p m", p=128)
        wk1t = wk1.rearrange("(k p) m -> k p m", p=128)
        xtmp_free = mod_free[1]
        x1t_free = [("VV", 0), ("VV", 0)]
        w1tmp_free = ("VV", 0)
        w2tmp_free = [("VV", 0)]
        w2p_free = [("TV", 0), ("TV", 0)]
        wvh_free = [("TV", 0), ("TV", 0)]
        wvraw_free = [mod_free[0], mod_free[1]]
        wos_free = [("TV", 0), ("TV", 0)]
        rwo = None
        for g in range(NG):
            own_lo = HALO if g == 0 else 0
            n_own = GT - own_lo
            own_glob = g * GT + own_lo - HALO
            prep_state["free"] = prep_state["free"] + [wos_free[1]]
            last_ev = 0
            rlast = None
            for tt in range(GT // 128):
                wait2("sync", xtmp_free,
                      *([wos_free[1]] if tt == 0 else []))
                dd = dma(XTMP, xw_t[(g * GT) // 128 + tt], sem="DLP")
                wait("tensor", "DLP", dd)
                for k in range(KD):
                    bank = 2 + (k % 2)
                    use_bank(bank)
                    rlast = tr(ps[bank][:, 0:128], XTMP[:, k * 128:(k + 1) * 128],
                               inc="TV")
                    wait("vector", "TV", rlast)
                    last_ev = vop(lambda e, k=k, tt=tt, b=bank: e.tensor_copy(
                        XFM(k)[:, tt * 128:(tt + 1) * 128], ps[b][:, 0:128]),
                        inc="VV")
                    bank_free[bank] = ("VV", last_ev)
                xtmp_free = ("TV", rlast)
                emit_prepass(1)
            wait("scalar", "VV", last_ev)
            sq_done = None
            for k in range(KD):
                sq_done = sop(lambda e, k=k: e.activation(
                    SQSC(k), XFM(k), AF.Square), inc="SC" if k == KD - 1 else None)
            wait("tensor", "SC", sq_done)
            use_bank(4)
            use_bank(5)
            r = None
            for k in range(KD):
                mm(ps[4][0:1, 0:GT], ONES[:, 0:1], XFM(k),
                   start=(k == 0), stop=(k == KD - 1))
                r = mm(ps[5][0:1, 0:GT], ONES[:, 0:1], SQSC(k),
                       start=(k == 0), stop=(k == KD - 1),
                       inc="TV" if k == KD - 1 else None)
            wait("vector", "TV", r)
            vop(lambda e: e.tensor_scalar_mul(R0[0:1, 0:GT], ps[4][0:1, 0:GT], 1.0 / E))
            vop(lambda e: e.tensor_scalar_mul(R1[0:1, 0:GT], ps[5][0:1, 0:GT], 1.0 / E))
            vop(lambda e: e.tensor_mul(R2[0:1, 0:GT], R0[0:1, 0:GT], R0[0:1, 0:GT]))
            vdrain()
            vop(lambda e: e.tensor_sub(R1[0:1, 0:GT], R1[0:1, 0:GT], R2[0:1, 0:GT]))
            vdrain()
            vst = vop(lambda e: e.tensor_scalar_add(R1[0:1, 0:GT], R1[0:1, 0:GT],
                                                    LN_EPS), inc="VV")
            bank_free[4] = ("VV", vst)
            bank_free[5] = ("VV", vst)
            wait("scalar", "VV", vst)
            ssq = sop(lambda e: e.activation(R1[0:1, 0:GT], R1[0:1, 0:GT], AF.Sqrt),
                      inc="SC")
            wait("vector", "SC", ssq)
            vop(lambda e: e.reciprocal(R1[0:1, 0:GT], R1[0:1, 0:GT]))
            vdrain()
            vst2 = vop(lambda e: e.tensor_mul(R2[0:1, 0:GT], R0[0:1, 0:GT],
                                              R1[0:1, 0:GT]), inc="VV")
            wait("tensor", "VV", vst2)
            for (row, dst) in [(R1, ISTD), (R2, MISTD)]:
                use_bank(6)
                r = mm(ps[6][:, 0:GT], ONES[0:1, :], row[0:1, 0:GT],
                       start=True, stop=True, inc="TV")
                wait("vector", "TV", r)
                ev = vop(lambda e, d=dst: e.tensor_copy(d, ps[6][:, 0:GT]), inc="VV")
                bank_free[6] = ("VV", ev)
            wait("vector", "VV", v_mod)
            h1v = None
            for k in range(KD):
                vop(lambda e, k=k: e.tensor_mul(SQSC(k), XFM(k), ISTD))
            for k in range(KD):
                vop(lambda e, k=k: e.tensor_sub(SQSC(k), SQSC(k), MISTD))
            for k in range(KD):
                vop(lambda e, k=k: e.tensor_scalar(
                    SQSC(k), SQSC(k), MOD[:, 16 + k:17 + k], MOD[:, k:k + 1],
                    ALU.mult, ALU.add))
            for k in range(KD):
                vop(lambda e, k=k: e.tensor_copy(H1R(k), SQSC(k)))
            for k in range(KD):
                h1v = vop(lambda e, k=k: e.tensor_sub(
                    H1LO(k), SQSC(k), H1R(k).bitcast(F32)),
                    inc="VV" if k == KD - 1 else None)
            if tap == "h1" and g == 0:
                wait("sync", "VV", h1v)
                finish(A2[:, 0:2 * KD * GT].bitcast(F32), 2 * KD * GT)
                return nc
            wait("vector", "VV", h1v)
            rg = None
            for (mat_t, dst_r, dst_lo, bank) in [(wq1t, G1QR, G1QLO, 2),
                                                 (wk1t, G1KR, G1KLO, 3)]:
                wv1 = None
                wait2("sync", wvh_free[0], wvh_free[1], ("TV", rg or 0))
                for k in range(KD):
                    wait2("sync", w1tmp_free)
                    dd = dma(W1TMP, mat_t[k], sem="DLX")
                    wait("vector", "DLX", dd)
                    if rg:
                        wait("vector", "TV", rg)
                    vop(lambda e, k=k: e.tensor_copy(W1Rv(k), W1TMP))
                    wv1 = vop(lambda e, k=k: e.tensor_sub(
                        W1LOv(k), W1TMP, W1Rv(k).bitcast(F32)), inc="VV")
                    w1tmp_free = ("VV", wv1)
                    emit_prepass(1)
                wait("tensor", "VV", wv1)
                use_bank(bank)
                for k in range(KD):
                    mm(ps[bank][:, 0:GT], W1Rv(k), H1R(k), start=(k == 0), stop=False)
                    mm(ps[bank][:, 0:GT], W1LOv(k), H1R(k), start=False, stop=False)
                    rg = mm(ps[bank][:, 0:GT], W1Rv(k), H1LO(k), start=False,
                            stop=(k == KD - 1), inc="TV" if k == KD - 1 else None)
                wait("vector", "TV", rg)
                vop(lambda e, d=dst_r, b=bank: e.tensor_copy(d, ps[b][:, 0:GT]))
                vdrain()
                g1d = vop(lambda e, dr=dst_r, dl=dst_lo, b=bank: e.tensor_sub(
                    dl, ps[b][:, 0:GT], dr.bitcast(F32)), inc="VV")
                bank_free[bank] = ("VV", g1d)
            wvh_free = [("TV", rg), ("TV", rg)]
            def emit_wv_dma(cc2):
                p2_ = cc2 % 2
                wait2("sync", wvraw_free[p2_])
                return dma(WVRAW(p2_).rearrange("p (a n) -> p a n", n=128),
                           wvr[cc2].rearrange("a p n -> p a n"),
                           sem="DLV0" if p2_ == 0 else "DLV1")

            def emit_w2_dma(cc2):
                wait2("sync", w2tmp_free[0])
                dma(W2TMP[:, 0:128], w2c_q[cc2], sem="DL2")
                return dma(W2TMP[:, 128:256], w2c_k[cc2], sem="DL2")

            d_wv = emit_wv_dma(0)
            d_w2 = emit_w2_dma(0)
            wait("vector", "DLV0", d_wv)
            wait2("vector", wvh_free[0])
            wait2("vector", wvh_free[1])
            wait("vector", "DL2", d_w2)
            wait2("vector", w2p_free[0])
            vop(lambda e: e.tensor_copy(WVHr(0), WVRAW(0)[:, 0:1024]))
            vop(lambda e: e.tensor_copy(WVHr(1), WVRAW(0)[:, 1024:2048]))
            vop(lambda e: e.tensor_copy(W2Rv(0), W2TMP))
            s0 = vop(lambda e: e.tensor_sub(WVHlo(0), WVRAW(0)[:, 0:1024],
                                            WVHr(0).bitcast(F32)), inc="VV")
            s1 = vop(lambda e: e.tensor_sub(WVHlo(1), WVRAW(0)[:, 1024:2048],
                                            WVHr(1).bitcast(F32)), inc="VV")
            w2v0 = vop(lambda e: e.tensor_sub(W2LOv(0), W2TMP,
                                              W2Rv(0).bitcast(F32)), inc="VV")
            spl_v = [s0, s1]
            w2split_v = w2v0
            wvraw_free[0] = ("VV", s1)
            w2tmp_free[0] = ("VV", w2v0)
            qv = None
            ss_last = None
            v_scan = 0
            for cc in range(KC):
                p = cc % 2
                pn = (cc + 1) % 2
                last_cc = (cc == KC - 1)
                if not last_cc:
                    d_wv = emit_wv_dma(cc + 1)
                    d_w2 = emit_w2_dma(cc + 1)
                use_bank(p)
                rvh = [None, None]
                for hh in range(2):
                    wait("tensor", "VV", spl_v[hh])
                    for kk in range(8):
                        k = hh * 8 + kk
                        first = (hh == 0 and kk == 0)
                        last = (hh == 1 and kk == 7)
                        mm(ps[p][:, 0:GT], WVHr(hh)[:, kk * 128:(kk + 1) * 128],
                           H1R(k), start=first, stop=False)
                        mm(ps[p][:, 0:GT], WVHlo(hh)[:, kk * 128:(kk + 1) * 128],
                           H1R(k), start=False, stop=False)
                        rvh[hh] = mm(ps[p][:, 0:GT],
                                     WVHr(hh)[:, kk * 128:(kk + 1) * 128],
                                     H1LO(k), start=False, stop=last,
                                     inc="TV" if kk == 7 else None)
                wait("tensor", "VV", w2split_v)
                use_bank(2)
                mm(ps[2][:, 0:GT], W2Rv(p)[:, 128:256], G1KR, start=True, stop=False)
                mm(ps[2][:, 0:GT], W2LOv(p)[:, 128:256], G1KR, start=False, stop=False)
                rk = mm(ps[2][:, 0:GT], W2Rv(p)[:, 128:256], G1KLO,
                        start=False, stop=True, inc="TV")
                use_bank(3)
                mm(ps[3][:, 0:n_own], W2Rv(p)[:, 0:128], G1QR[:, own_lo:GT],
                   start=True, stop=False)
                mm(ps[3][:, 0:n_own], W2LOv(p)[:, 0:128], G1QR[:, own_lo:GT],
                   start=False, stop=False)
                rq = mm(ps[3][:, 0:n_own], W2Rv(p)[:, 0:128], G1QLO[:, own_lo:GT],
                        start=False, stop=True, inc="TV")
                w2p_free[p] = ("TV", rq)
                wait("scalar", "TV", rk)
                if v_scan:
                    wait("scalar", "VV", v_scan)
                sop(lambda e: e.activation(FB, ps[2][:, 0:GT], AF.Sigmoid))
                wait("scalar", "TV", rq)
                sqq_prev = qv
                if sqq_prev:
                    wait("scalar", "VV", sqq_prev)
                sop(lambda e, n=n_own: e.activation(
                    SQB[:, 0:n], ps[3][:, 0:n], AF.Sigmoid))
                sa = sop(lambda e: e.activation(AB, FB, AF.Sigmoid), inc="SC")
                wait("vector", "TV", rvh[0])
                wait("vector", "SC", sa)
                vop(lambda e: e.tensor_scalar(KB, AB, -1.0, 1.0, ALU.mult, ALU.add))
                if not last_cc:
                    wait("vector", "DLV0" if pn == 0 else "DLV1", d_wv)
                    vop(lambda e, pn=pn: e.tensor_copy(
                        WVHr(0), WVRAW(pn)[:, 0:1024]))
                else:
                    vdrain()
                wait("vector", "TV", rvh[1])
                vop(lambda e, p=p: e.tensor_mul(BB, KB, ps[p][:, 0:GT]))
                if not last_cc:
                    wait("vector", "TV", rvh[1])
                    vop(lambda e, pn=pn: e.tensor_copy(
                        WVHr(1), WVRAW(pn)[:, 1024:2048]))
                else:
                    vdrain()
                vb = None
                if g == 0:
                    vb = vop(lambda e: e.tensor_mul(BB, BB, MASK), inc="VV")
                if not last_cc:
                    wait("vector", "DL2", d_w2)
                    wait2("vector", w2p_free[pn])
                    vop(lambda e, pn=pn: e.tensor_copy(W2Rv(pn), W2TMP))
                else:
                    vdrain()
                vop(lambda e, cc=cc: e.tensor_tensor_scan(
                    HB, AB, BB, HCAR[:, cc:cc + 1], ALU.mult, ALU.add))
                if vb is None:
                    vb = C["VV"]
                if not last_cc:
                    w0 = vop(lambda e, pn=pn: e.tensor_sub(
                        WVHlo(0), WVRAW(pn)[:, 0:1024], WVHr(0).bitcast(F32)),
                        inc="VV")
                else:
                    vdrain()
                    w0 = None
                v_scan = vop(lambda e, cc=cc: e.tensor_copy(
                    HCAR[:, cc:cc + 1], HB[:, GT - 1:GT]), inc="VV")
                bank_free[p] = ("VV", v_scan)
                sqa = vop(lambda e, n=n_own: e.tensor_mul(
                    SQB[:, 0:n], SQB[:, 0:n], ps[3][:, 0:n]), inc="VV")
                bank_free[3] = ("VV", sqa)
                if not last_cc:
                    w1 = vop(lambda e, pn=pn: e.tensor_sub(
                        WVHlo(1), WVRAW(pn)[:, 1024:2048], WVHr(1).bitcast(F32)),
                        inc="VV")
                    wvraw_free[pn] = ("VV", w1)
                else:
                    vdrain()
                    w1 = None
                vop(lambda e, n=n_own, o=own_lo: e.tensor_mul(
                    SQB[:, 0:n], SQB[:, 0:n], HB[:, o:GT]))
                if not last_cc:
                    w2s = vop(lambda e, pn=pn: e.tensor_sub(
                        W2LOv(pn), W2TMP, W2Rv(pn).bitcast(F32)), inc="VV")
                    w2tmp_free[0] = ("VV", w2s)
                    w2split_v = w2s
                else:
                    vdrain()
                qv = vop(lambda e, cc=cc, n=n_own: e.tensor_scalar(
                    QH(cc, n), SQB[:, 0:n], NW[:, cc:cc + 1], None, ALU.mult),
                    inc="VV")
                if not last_cc:
                    spl_v = [w0, w1]
                wait("scalar", "VV", qv)
                scq = sop(lambda e, n=n_own: e.activation(
                    KB[:, 0:n], SQB[:, 0:n], AF.Square), inc="SC")
                wait("tensor", "SC", scq)
                if cc == 0:
                    use_bank(4)
                ss_last = mm(ps[4][0:1, 0:n_own], ONES[:, 0:1], KB[:, 0:n_own],
                             start=(cc == 0), stop=(cc == KC - 1), inc="TV")
                emit_prepass(1)
            wvh_free = [("TV", rvh[0]), ("TV", rvh[1])]
            if tap == "scan" and g == 0:
                wait("sync", "VV", qv)
                finish(fv(A1, 0, 32 * n_own), 32 * n_own)
                return nc
            wait("vector", "TV", ss_last)
            vsr = vop(lambda e, n=n_own: e.tensor_scalar(
                R0[0:1, 0:n], ps[4][0:1, 0:n], 1.0 / E2, RMS_EPS, ALU.mult,
                ALU.add), inc="VV")
            bank_free[4] = ("VV", vsr)
            wait("scalar", "VV", vsr)
            ssq2 = sop(lambda e, n=n_own: e.activation(R0[0:1, 0:n], R0[0:1, 0:n],
                                                       AF.Sqrt), inc="SC")
            wait("vector", "SC", ssq2)
            vrr = vop(lambda e, n=n_own: e.reciprocal(R0[0:1, 0:n], R0[0:1, 0:n]),
                      inc="VV")
            wait("tensor", "VV", vrr)
            use_bank(6)
            rb = mm(ps[6][:, 0:n_own], ONES[0:1, :], R0[0:1, 0:n_own],
                    start=True, stop=True, inc="TV")
            wait("vector", "TV", rb)
            erb = vop(lambda e, n=n_own: e.tensor_copy(RSTD[:, 0:n],
                                                       ps[6][:, 0:n]), inc="VV")
            bank_free[6] = ("VV", erb)
            wait("vector", "VV", erb)
            emit_prep_consume()
            wos_req = [wvraw_free[0], wvraw_free[1], wos_free[0], wos_free[1],
                       *prep_state["free"], xtmp_free]
            for mc in range(KD):
                p2 = mc % 2
                if mc < 2:
                    wait2("sync", *wos_req)
                else:
                    wait2("sync", wos_free[p2])
                dwo = dma(WOS(p2).rearrange("p (a n) -> p a n", n=128),
                          wor[mc].rearrange("a p n -> p a n"),
                          sem="DLWA" if p2 == 0 else "DLWB")
                bank = 5 + p2
                wait("tensor", "DLWA" if p2 == 0 else "DLWB", dwo)
                use_bank(bank)
                rwo = None
                for cc in range(KC):
                    rwo = mm(ps[bank][:, 0:n_own],
                             WOS(p2)[:, cc * 128:(cc + 1) * 128], QH(cc, n_own),
                             start=(cc == 0), stop=(cc == KC - 1),
                             inc="TV" if cc == KC - 1 else None)
                wos_free[p2] = ("TV", rwo)
                wait("vector", "TV", rwo)
                wait2("vector", x1t_free[p2])
                vop(lambda e, b=bank, n=n_own, p2=p2: e.tensor_mul(
                    X1T(p2)[:, 0:n], ps[b][:, 0:n], RSTD[:, 0:n]))
                vdrain()
                vx1 = vop(lambda e, mc=mc, n=n_own, o=own_lo, p2=p2:
                          e.scalar_tensor_tensor(
                              X1T(p2)[:, 0:n], X1T(p2)[:, 0:n],
                              MOD[:, 32 + mc:33 + mc], XFM(mc)[:, o:GT],
                              ALU.mult, ALU.add), inc="VV")
                bank_free[bank] = ("VV", vx1)
                wait("sync", "VV", vx1)
                stx = dma(x1s[:, mc * OWN + own_glob:mc * OWN + own_glob + n_own],
                          X1T(p2)[:, 0:n_own], sem="STA" if p2 == 0 else "STB")
                x1t_free[p2] = ("STA" if p2 == 0 else "STB", stx)
            wvraw_free = [("TV", rwo), ("TV", rwo)]
        prep_state["free"] = prep_state["free"] + [wos_free[1]]
        emit_prepass(3 * KE)
        emit_prep_consume()
        scales_done = emit_scales()
        tv_attn = C["TV"]
        vv_attn = C["VV"]
        if tap == "x1":
            wait("sync", "VV", scales_done)
            wait("sync", "STA", C["STA"])
            wait("sync", "STB", C["STB"])
            dwb = dma(A4[:, 0:8192], x1s[:, 0:8192])
            wait("sync", "DLX", dwb)
            finish(A4[:, 0:8192], 8192)
            return nc

        x2_free = [("TV", tv_attn), ("VV", vv_attn),
                   ("STA", C["STA"]), ("STB", C["STB"])]
        wfl_free = ("TV", tv_attn)
        hid_free = [("TV", tv_attn)] * 4
        tds_free = [("TV", tv_attn)] * 4
        tug_free = ("TV", tv_attn)
        gts_free = ("TV", tv_attn)
        dacc_v = [0] * KD
        rdn = None
        for h in range(NH):
            if isinstance(x2_free, list):
                wait2("sync", *x2_free)
            else:
                wait2("sync", x2_free)
            dx = None
            for mc in range(KD):
                dx = dma(X2H(mc), x1s[:, mc * OWN + h * HT:mc * OWN + (h + 1) * HT])
            wait("scalar", "DLX", dx)
            if h == 0:
                wait("scalar", "TV", tv_attn)
            sq_done = None
            for k in range(KD):
                sq_done = sop(lambda e, k=k: e.activation(
                    H2SQ(k), X2H(k), AF.Square), inc="SC" if k == KD - 1 else None)
            wait("tensor", "SC", sq_done)
            use_bank(4)
            use_bank(5)
            r = None
            for k in range(KD):
                mm(ps[4][0:1, 0:HT], ONES[:, 0:1], X2H(k),
                   start=(k == 0), stop=(k == KD - 1))
                r = mm(ps[5][0:1, 0:HT], ONES[:, 0:1], H2SQ(k),
                       start=(k == 0), stop=(k == KD - 1),
                       inc="TV" if k == KD - 1 else None)
            wait("vector", "TV", r)
            vop(lambda e: e.tensor_scalar_mul(R0[0:1, 0:HT], ps[4][0:1, 0:HT], 1.0 / E))
            vop(lambda e: e.tensor_scalar_mul(R1[0:1, 0:HT], ps[5][0:1, 0:HT], 1.0 / E))
            vop(lambda e: e.tensor_mul(R2[0:1, 0:HT], R0[0:1, 0:HT], R0[0:1, 0:HT]))
            vdrain()
            vop(lambda e: e.tensor_sub(R1[0:1, 0:HT], R1[0:1, 0:HT], R2[0:1, 0:HT]))
            vdrain()
            vst = vop(lambda e: e.tensor_scalar_add(R1[0:1, 0:HT], R1[0:1, 0:HT],
                                                    LN_EPS), inc="VV")
            bank_free[4] = ("VV", vst)
            bank_free[5] = ("VV", vst)
            wait("scalar", "VV", vst)
            ssq = sop(lambda e: e.activation(R1[0:1, 0:HT], R1[0:1, 0:HT], AF.Sqrt),
                      inc="SC")
            wait("vector", "SC", ssq)
            vop(lambda e: e.reciprocal(R1[0:1, 0:HT], R1[0:1, 0:HT]))
            vdrain()
            vst2 = vop(lambda e: e.tensor_mul(R2[0:1, 0:HT], R0[0:1, 0:HT],
                                              R1[0:1, 0:HT]), inc="VV")
            wait("tensor", "VV", vst2)
            for (row, dst) in [(R1, I2), (R2, M2)]:
                use_bank(6)
                r = mm(ps[6][:, 0:HT], ONES[0:1, :], row[0:1, 0:HT],
                       start=True, stop=True, inc="TV")
                wait("vector", "TV", r)
                ev = vop(lambda e, d=dst: e.tensor_copy(d, ps[6][:, 0:HT]), inc="VV")
                bank_free[6] = ("VV", ev)
            wait("vector", "VV", scales_done)
            h2v = None
            for k in range(KD):
                vop(lambda e, k=k: e.tensor_mul(H2SQ(k), X2H(k), I2))
            for k in range(KD):
                vop(lambda e, k=k: e.tensor_sub(H2SQ(k), H2SQ(k), M2))
            for k in range(KD):
                vop(lambda e, k=k: e.tensor_scalar(
                    H2SQ(k), H2SQ(k), MOD[:, 64 + k:65 + k], MOD[:, 48 + k:49 + k],
                    ALU.mult, ALU.add))
            for k in range(KD):
                vop(lambda e, k=k: e.tensor_copy(H2R(k), H2SQ(k)))
            for k in range(KD):
                h2v = vop(lambda e, k=k: e.tensor_sub(
                    H2LO(k), H2SQ(k), H2R(k).bitcast(F32)),
                    inc="VV" if k == KD - 1 else None)
            if tap == "h2" and h == 0:
                wait("sync", "VV", h2v)
                finish(A1[:, 0:8192].bitcast(F32), 8192)
                return nc
            wait2("vector", wos_free[0], wos_free[1], *prep_state["free"],
                  wvraw_free[0], wvraw_free[1])
            dz = vop(lambda e: e.memset(A4[:, 0:8192], 0.0), inc="VV")
            for mc in range(KD):
                dacc_v[mc] = dz
            for eg in range(KE // 4):
                hv = None
                for j in range(4):
                    ec = eg * 4 + j
                    tv_list = []
                    for (mat, dst_t, scix) in [(wupr, 0, 3), (wgater, 1, 4),
                                               (wdownr, 2, 5)]:
                        wait2("sync", wfl_free)
                        dd = dma(WFL.rearrange("p (a n) -> p a n", n=128),
                                 mat[ec].rearrange("a p n -> p a n"), sem="DLF")
                        wait("vector", "DLF", dd)
                        if dst_t == 2:
                            wait2("vector", tds_free[j])
                        else:
                            wait2("vector", tug_free)
                        wait2("vector", gts_free)
                        dap = (TUG(dst_t) if dst_t < 2 else TDS(j))
                        tv = None
                        for pair in range(2):
                            c0, c1 = pair * 2, pair * 2 + 1
                            for cch in (c0, c1):
                                g0_ = (cch % 2) * 512
                                vop(lambda e, cch=cch, g0_=g0_, scix=scix:
                                    e.tensor_scalar(
                                        GTSC[:, g0_:g0_ + 512],
                                        WFL[:, cch * 512:(cch + 1) * 512],
                                        SCOL[:, scix:scix + 1], None, ALU.is_gt))
                                vop(lambda e, cch=cch, g0_=g0_, scix=scix:
                                    e.tensor_scalar(
                                        LTSC[:, g0_:g0_ + 512],
                                        WFL[:, cch * 512:(cch + 1) * 512],
                                        SCOL[:, scix + 3:scix + 4], None,
                                        ALU.is_lt))
                            for cch in (c0, c1):
                                g0_ = (cch % 2) * 512
                                tv = vop(lambda e, cch=cch, g0_=g0_, dap=dap:
                                         e.tensor_sub(
                                             dap[:, cch * 512:(cch + 1) * 512],
                                             GTSC[:, g0_:g0_ + 512],
                                             LTSC[:, g0_:g0_ + 512]),
                                         inc="VV" if cch == c1 else None)
                        wfl_free = ("VV", tv)
                        gts_free = ("VV", tv)
                        tv_list.append(tv)
                    wait("tensor", "VV", tv_list[1])
                    wait("tensor", "VV", h2v)
                    use_bank(0)
                    use_bank(1)
                    ru = None
                    for k in range(KD):
                        mm(ps[0][:, 0:HT], TUG(0)[:, k * 128:(k + 1) * 128], H2R(k),
                           start=(k == 0), stop=False)
                        mm(ps[0][:, 0:HT], TUG(0)[:, k * 128:(k + 1) * 128], H2LO(k),
                           start=False, stop=(k == KD - 1))
                        mm(ps[1][:, 0:HT], TUG(1)[:, k * 128:(k + 1) * 128], H2R(k),
                           start=(k == 0), stop=False)
                        ru = mm(ps[1][:, 0:HT], TUG(1)[:, k * 128:(k + 1) * 128],
                                H2LO(k), start=False, stop=(k == KD - 1),
                                inc="TV" if k == KD - 1 else None)
                    tug_free = ("TV", ru)
                    wait("scalar", "TV", ru)
                    sg = sop(lambda e: e.activation(SGB, ps[0][:, 0:HT], AF.Sigmoid,
                                                    scale=SCOL[:, 0:1]), inc="SC")
                    wait("vector", "SC", sg)
                    wait2("vector", hid_free[j])
                    vop(lambda e: e.tensor_mul(T1B, SGB, ps[0][:, 0:HT]))
                    vdrain()
                    vop(lambda e: e.tensor_mul(HPL, T1B, ps[1][:, 0:HT]))
                    vdrain()
                    vop(lambda e, j=j: e.tensor_copy(HIDR(j), HPL))
                    vdrain()
                    hv = vop(lambda e, j=j: e.tensor_sub(
                        HIDLO(j), HPL, HIDR(j).bitcast(F32)), inc="VV")
                    bank_free[0] = ("VV", hv)
                    bank_free[1] = ("VV", hv)
                wait("tensor", "VV", hv)
                for mc in range(KD):
                    bank = 2 + mc % 2
                    use_bank(bank)
                    for j in range(4):
                        mm(ps[bank][:, 0:HT], TDS(j)[:, mc * 128:(mc + 1) * 128],
                           HIDR(j), start=(j == 0), stop=False)
                        rdn = mm(ps[bank][:, 0:HT],
                                 TDS(j)[:, mc * 128:(mc + 1) * 128], HIDLO(j),
                                 start=False, stop=(j == 3),
                                 inc="TV" if j == 3 else None)
                    wait("vector", "TV", rdn)
                    wait("vector", "VV", dacc_v[mc])
                    dacc_v[mc] = vop(lambda e, mc=mc, b=bank: e.tensor_add(
                        DACC(mc), DACC(mc), ps[b][:, 0:HT]), inc="VV")
                    bank_free[bank] = ("VV", dacc_v[mc])
                for j in range(4):
                    hid_free[j] = ("TV", rdn)
                    tds_free[j] = ("TV", rdn)
            wait("sync", "TV", rdn)
            dx2 = None
            for mc in range(KD):
                dx2 = dma(X2H(mc), x1s[:, mc * OWN + h * HT:mc * OWN + (h + 1) * HT])
            wait("vector", "DLX", dx2)
            xv = None
            for mc in range(KD):
                wait("vector", "VV", dacc_v[mc])
                xv = vop(lambda e, mc=mc: e.scalar_tensor_tensor(
                    X2H(mc), DACC(mc), GBS[:, mc:mc + 1], X2H(mc),
                    ALU.mult, ALU.add), inc="VV")
            wait("tensor", "VV", xv)
            xo_free = ("VV", 0)
            for tt in range(HT // 128):
                evo = None
                for mc in range(KD):
                    bank = 6 + mc % 2
                    use_bank(bank)
                    rto = tr(ps[bank][:, 0:128],
                             X2H(mc)[:, tt * 128:(tt + 1) * 128], inc="TV")
                    wait("vector", "TV", rto)
                    wait2("vector", xo_free)
                    evo = vop(lambda e, mc=mc, b=bank: e.tensor_copy(
                        XO[:, mc * 128:(mc + 1) * 128], ps[b][:, 0:128]), inc="VV")
                    bank_free[bank] = ("VV", evo)
                wait("sync", "VV", evo)
                row0 = h * HT + tt * 128
                dma(xout[row0:row0 + 128, :], XO, sem="STO")
                xo_free = ("STO", C["STO"])
            x2_free = [("STO", C["STO"]), ("STA", C["STA"]), ("STB", C["STB"])]
            wfl_free = ("STO", C["STO"])

        finish()
        return nc


def emit_all(block, STR):
    @block.sync
    def _(sync):
        for f in STR["sync"]:
            f(sync)

    @block.tensor
    def _(tensor):
        for f in STR["tensor"]:
            f(tensor)

    @block.vector
    def _(vector):
        for f in STR["vector"]:
            f(vector)

    @block.scalar
    def _(scalar):
        for f in STR["scalar"]:
            f(scalar)


_NC_CACHE = {}


def _get_nc(tap=None):
    if tap not in _NC_CACHE:
        _NC_CACHE[tap] = build_nc(tap)
    return _NC_CACHE[tap]


def stage_inputs(inputs):
    f32 = np.float32
    x = np.ascontiguousarray(np.asarray(inputs["L0_attn_x"], dtype=f32))
    c = np.ascontiguousarray(np.asarray(inputs["L0_c"], dtype=f32))

    wv_r = np.ascontiguousarray(np.asarray(
        inputs["L0_attn_Wv"], dtype=f32).reshape(KD, 128, KC, 128).transpose(2, 0, 1, 3))
    wo_r = np.ascontiguousarray(np.asarray(
        inputs["L0_attn_Wo"], dtype=f32).reshape(KC, 128, KD, 128).transpose(2, 0, 1, 3))
    wup_r = np.ascontiguousarray(np.asarray(
        inputs["L0_ffn_W_up"], dtype=f32).T.reshape(KD, 128, KE, 128).transpose(2, 0, 1, 3))
    wgate_r = np.ascontiguousarray(np.asarray(
        inputs["L0_ffn_W_gate"], dtype=f32).T.reshape(KD, 128, KE, 128).transpose(2, 0, 1, 3))
    wdown_r = np.ascontiguousarray(np.asarray(
        inputs["L0_ffn_W_down"], dtype=f32).T.reshape(KE, 128, KD, 128).transpose(0, 2, 1, 3))
    wmod_r = np.ascontiguousarray(np.asarray(
        inputs["L0_wmod"], dtype=f32).reshape(KD, 128, 24, 512).transpose(2, 0, 1, 3))
    normw_r = np.ascontiguousarray(np.asarray(
        inputs["L0_attn_norm_w"], dtype=f32).reshape(KC, 128).T)
    idt = np.eye(128, dtype=f32)
    wq1 = np.ascontiguousarray(np.asarray(inputs["L0_attn_Wq1"], dtype=f32))
    wk1 = np.ascontiguousarray(np.asarray(inputs["L0_attn_Wk1"], dtype=f32))
    wq2 = np.ascontiguousarray(np.asarray(inputs["L0_attn_Wq2"], dtype=f32))
    wk2 = np.ascontiguousarray(np.asarray(inputs["L0_attn_Wk2"], dtype=f32))

    in_maps = []
    for core in range(8):
        b, q = core // 4, core % 4
        if q == 0:
            xw = np.concatenate([x[b, 0:HALO], x[b, 0:OWN]], axis=0)
            mk = np.ones((1, GT), dtype=f32)
            mk[0, 0:HALO] = 0.0
        else:
            xw = x[b, q * OWN - HALO:(q + 1) * OWN]
            mk = np.ones((1, GT), dtype=f32)
        mask = np.ascontiguousarray(np.broadcast_to(mk, (128, GT)))
        cre = np.ascontiguousarray(c[b].reshape(KD, 128).T)
        in_maps.append({
            "xwin": np.ascontiguousarray(xw), "maskr": mask, "cresh": cre,
            "wq1": wq1, "wk1": wk1, "wq2": wq2, "wk2": wk2,
            "wvr": wv_r, "wor": wo_r, "wupr": wup_r, "wgater": wgate_r,
            "wdownr": wdown_r, "wmodr": wmod_r, "normwr": normw_r,
            "idt_in": idt,
        })
    return in_maps


def kernel(**inputs):
    in_maps = stage_inputs(inputs)
    nc = _get_nc()
    res = run_bass_kernel_spmd(nc, in_maps, core_ids=list(range(8)))
    out = np.empty((2, 4096, E), dtype=np.float32)
    for core in range(8):
        b, q = core // 4, core % 4
        out[b, q * OWN:(q + 1) * OWN, :] = res.results[core]["xout"]
    return out


# version 06 (final)
# speedup vs baseline: 1.3624x; 1.2736x over previous
import numpy as np
from contextlib import ExitStack

import concourse.bass as bass
import concourse.mybir as mybir
import concourse.bass_utils as _bass_utils
from concourse.bass_utils import run_bass_kernel_spmd

_orig_run_command = _bass_utils.run_command


def _run_command_no_birverifier(argv, **kwargs):
    argv = [a.replace("birverifier,", "") if isinstance(a, str) else a
            for a in argv]
    return _orig_run_command(argv, **kwargs)


_bass_utils.run_command = _run_command_no_birverifier

F32 = mybir.dt.float32
F32R = mybir.dt.float32r
AF = mybir.ActivationFunctionType
ALU = mybir.AluOpType

E = 2048
E2 = 4096
DF = 8192
HD = 128
OWN = 1024
HALO = 128
T = OWN + HALO
KD = E // 128
KC = E2 // 128
KE = DF // 128
NG = 3
GT = T // NG
NH = 2
HT = OWN // NH
LN_EPS = 1e-5
RMS_EPS = 1e-6
Q_EPS = 1e-8


def build_nc(tap=None, serial_dma=False, race_detect=True):
    nc = bass.Bass("TRN2", target_bir_lowering=False,
                   detect_race_conditions=race_detect)

    xwin = nc.dram_tensor("xwin", [T, E], F32, kind="ExternalInput").ap()
    maskr = nc.dram_tensor("maskr", [128, GT], F32, kind="ExternalInput").ap()
    cresh = nc.dram_tensor("cresh", [128, KD], F32, kind="ExternalInput").ap()
    wq1 = nc.dram_tensor("wq1", [E, HD], F32, kind="ExternalInput").ap()
    wk1 = nc.dram_tensor("wk1", [E, HD], F32, kind="ExternalInput").ap()
    wq2 = nc.dram_tensor("wq2", [HD, E2], F32, kind="ExternalInput").ap()
    wk2 = nc.dram_tensor("wk2", [HD, E2], F32, kind="ExternalInput").ap()
    wvr = nc.dram_tensor("wvr", [KC, KD, 128, 128], F32, kind="ExternalInput").ap()
    wor = nc.dram_tensor("wor", [KD, KC, 128, 128], F32, kind="ExternalInput").ap()
    wupr = nc.dram_tensor("wupr", [KE, KD, 128, 128], F32, kind="ExternalInput").ap()
    wgater = nc.dram_tensor("wgater", [KE, KD, 128, 128], F32, kind="ExternalInput").ap()
    wdownr = nc.dram_tensor("wdownr", [KE, KD, 128, 128], F32, kind="ExternalInput").ap()
    wmodr = nc.dram_tensor("wmodr", [24, KD, 128, 512], F32, kind="ExternalInput").ap()
    normwr = nc.dram_tensor("normwr", [128, KC], F32, kind="ExternalInput").ap()
    idt_in = nc.dram_tensor("idt_in", [128, 128], F32, kind="ExternalInput").ap()
    xout = nc.dram_tensor("xout", [OWN, E], F32, kind="ExternalOutput").ap()
    x1s = nc.dram_tensor("x1s", [128, KD * OWN], F32).ap()

    with ExitStack() as ctx:
        en = ctx.enter_context

        A1 = en(nc.sbuf_tensor("A1", [128, 12288], F32R))
        A2 = en(nc.sbuf_tensor("A2", [128, 12288], F32R))
        A3 = en(nc.sbuf_tensor("A3", [128, 8192], F32R))
        A4 = en(nc.sbuf_tensor("A4", [128, 8192], F32))
        A5 = en(nc.sbuf_tensor("A5", [128, 5120], F32R))
        A6 = en(nc.sbuf_tensor("A6", [128, 4608], F32R))
        A7 = en(nc.sbuf_tensor("A7", [128, 1536], F32))
        MKC = en(nc.sbuf_tensor("MKC", [128, 640], F32))

        def fv(t, a, b):
            return t[:, a:b].bitcast(F32)

        def QH(cc, n):
            return fv(A1, cc * n, (cc + 1) * n)

        def SQSC(k):
            return fv(A1, k * GT, (k + 1) * GT)

        def H1R(k):
            return A2[:, k * GT:(k + 1) * GT]

        def H1LO(k):
            return A2[:, KD * GT + k * GT:KD * GT + (k + 1) * GT]

        def XFM(k):
            return fv(A3, k * GT, (k + 1) * GT)
        RSTD = fv(A3, 6144, 6144 + GT)

        def X1T(par):
            return fv(A3, 6528 + par * GT, 6528 + (par + 1) * GT)

        def WVRAW(p):
            return A4[:, p * 2048:(p + 1) * 2048]
        XTMP = A4[:, 4096:6144]

        def WOS(p):
            return A4[:, p * 4096:(p + 1) * 4096]
        PREP = A4[:, 6144:8192]

        def WVHr(hh):
            return A5[:, hh * 2048:hh * 2048 + 1024]

        def WVHlo(hh):
            return A5[:, hh * 2048 + 1024:(hh + 1) * 2048]

        def W2Rv(p):
            return A5[:, 4096 + p * 512:4096 + p * 512 + 256]

        def W2LOv(p):
            return A5[:, 4096 + p * 512 + 256:4096 + (p + 1) * 512]

        def W1Rv(k):
            return A5[:, k * 128:(k + 1) * 128]

        def W1LOv(k):
            return A5[:, 2048 + k * 128:2048 + (k + 1) * 128]

        G1QR = A6[:, 0:GT]
        G1QLO = A6[:, GT:2 * GT]
        G1KR = A6[:, 2 * GT:3 * GT]
        G1KLO = A6[:, 3 * GT:4 * GT]
        FB = fv(A6, 4 * GT, 5 * GT)
        AB = fv(A6, 5 * GT, 6 * GT)
        KB = fv(A6, 6 * GT, 7 * GT)
        BB = fv(A6, 7 * GT, 8 * GT)
        HB = fv(A6, 8 * GT, 9 * GT)
        SQB = fv(A6, 9 * GT, 10 * GT)
        ISTD = FB
        MISTD = AB
        W1TMP = fv(A6, 3840, 3968)
        W2TMP = fv(A6, 3968, 4224)
        CRE = fv(A6, 4224, 4240)
        SC_ = fv(A6, 4240, 4256)
        MOD = fv(A6, 4256, 4352)
        NW = fv(A6, 4352, 4384)
        HCAR = fv(A6, 4384, 4416)
        SCOL = fv(A6, 4416, 4432)
        ABSC = fv(A6, 4432, 4444)
        GBS = fv(A6, 4444, 4460)
        ROWT = fv(A6, 4460, 4464)
        TT1 = fv(A6, 4464, 4480)
        TT2 = fv(A6, 4480, 4496)
        TT3 = fv(A6, 4496, 4512)
        TT4 = fv(A6, 4512, 4528)
        TT5 = fv(A6, 4528, 4544)
        TT6 = fv(A6, 4544, 4560)
        R0 = A7[0:1, 0:512]
        R1 = A7[0:1, 512:1024]
        R2 = A7[0:1, 1024:1536]
        MASK = fv(MKC, 0, GT)
        IDT = MKC[:, 384:512]
        ONES = MKC[:, 512:640]
        def H2R(k):
            return A2[:, k * HT:(k + 1) * HT]

        def H2SQ(k):
            return fv(A1, k * HT, (k + 1) * HT)

        def H2LO(k):
            return A1[:, k * HT:(k + 1) * HT]

        def X2H(k):
            return fv(A3, k * HT, (k + 1) * HT)

        def HIDR(j):
            return A3[:, j * 1024:j * 1024 + 512]

        def HIDLO(j):
            return A3[:, j * 1024 + 512:(j + 1) * 1024]

        def TUG(m):
            return A3[:, 4096 + m * 2048:4096 + (m + 1) * 2048]

        def DACC(mc):
            return A4[:, mc * HT:(mc + 1) * HT]

        def TDS(j):
            return [A5[:, 0:2048], A5[:, 2048:4096],
                    A1[:, 9216:11264], A2[:, 9216:11264]][j]
        GTSC = fv(A1, 8192, 9216)
        LTSC = fv(A2, 8192, 9216)
        WFB1 = fv(A1, 11264, 12288)
        WFB2 = fv(A2, 11264, 12288)
        WFL = fv(A6, 0, 2048)
        SGB = fv(A6, 2048, 2560)
        T1B = fv(A6, 2560, 3072)
        HPL = fv(A6, 3072, 3584)
        XO = fv(A6, 0, 2048)
        I2 = A4[:, 0:512]
        M2 = A4[:, 512:1024]

        ps = [en(nc.psum_tensor(f"ps{i}", [128, 512], F32)) for i in range(8)]

        hs = {}
        for s in ["DLX", "DLV0", "DLV1", "DL2", "DLWA", "DLWB", "DLP", "DLF",
                  "DLF1", "TV", "VV", "SC", "STA", "STB", "STO"]:
            hs[s] = en(nc.semaphore(s))
        C = {k: 0 for k in hs}

        block = en(nc.Block())
        STR = {"sync": [], "tensor": [], "vector": [], "scalar": []}

        def wait(stream, sem, val):
            if val and val > 0:
                STR[stream].append(lambda e, h=hs[sem], v=val: e.wait_ge(h, v))

        def wait2(stream, *svs):
            best = {}
            for sv in svs:
                if sv is None:
                    continue
                s, v = sv
                if v and v > 0:
                    best[s] = max(best.get(s, 0), v)
            for s, v in best.items():
                wait(stream, s, v)

        def dma(dst, src, sem="DLX"):
            C[sem] += 16
            STR["sync"].append(lambda e, d=dst, s=src, h=hs[sem]:
                               e.dma_start(out=d, in_=s).then_inc(h, 16))
            if serial_dma:
                wait("sync", sem, C[sem])
            return C[sem]

        def mm(psap, lhsT, rhs, start, stop, inc=None):
            if inc is None:
                STR["tensor"].append(lambda e, o=psap, a=lhsT, b=rhs, s1=start,
                                     s2=stop: e.matmul(o, a, b, start=s1, stop=s2))
                return None
            C[inc] += 1
            STR["tensor"].append(lambda e, o=psap, a=lhsT, b=rhs, s1=start, s2=stop,
                                 h=hs[inc]: e.matmul(o, a, b, start=s1,
                                                     stop=s2).then_inc(h, 1))
            return C[inc]

        def tr(psap, inap, inc=None):
            if inc is None:
                STR["tensor"].append(lambda e, o=psap, a=inap: e.transpose(o, a, IDT))
                return None
            C[inc] += 1
            STR["tensor"].append(lambda e, o=psap, a=inap, h=hs[inc]:
                                 e.transpose(o, a, IDT).then_inc(h, 1))
            return C[inc]

        def vop(fn, inc=None):
            if inc is None:
                STR["vector"].append(fn)
                return None
            C[inc] += 1
            STR["vector"].append(lambda e, f=fn, h=hs[inc]: f(e).then_inc(h, 1))
            return C[inc]

        def sop(fn, inc=None):
            if inc is None:
                STR["scalar"].append(fn)
                return None
            C[inc] += 1
            STR["scalar"].append(lambda e, f=fn, h=hs[inc]: f(e).then_inc(h, 1))
            return C[inc]

        def vdrain():
            STR["vector"].append(lambda e: e.drain())

        bank_free = [("VV", 0)] * 8

        def use_bank(i):
            wait2("tensor", bank_free[i])

        def finish(extra_src=None, extra_n=0):
            if extra_src is not None:
                xv = xout.rearrange("(p a) d -> p (a d)", p=128)
                dma(xv[:, 0:extra_n], extra_src, sem="STO")
            for st_ in ("STA", "STB", "STO"):
                wait("sync", st_, C[st_])
            emit_all(block, STR)

        dma(IDT, idt_in)
        dma(CRE, cresh)
        dma(NW, normwr)
        d0 = dma(MASK, maskr)
        wait("vector", "DLX", d0)
        vop(lambda e: e.memset(ONES, 1.0))
        vop(lambda e: e.memset(HCAR, 0.0))
        v_ones = vop(lambda e: e.memset(ABSC, 0.0), inc="VV")
        wait("scalar", "DLX", d0)
        wait("scalar", "VV", v_ones)
        c_sg = sop(lambda e: e.activation(SC_, CRE, AF.Sigmoid), inc="SC")
        wait("vector", "SC", c_sg)
        v_sc = vop(lambda e: e.tensor_mul(SC_, SC_, CRE), inc="VV")

        wait("tensor", "VV", v_sc)
        wait("tensor", "DLX", d0)
        mod_free = [("TV", 0), ("TV", 0)]
        r = None
        for jt in range(24):
            for half in range(2):
                p = half
                wait2("sync", mod_free[p])
                dd = dma(WOS(p).rearrange("p (a n) -> p a n", n=512),
                         wmodr[jt, half * 8:(half + 1) * 8].rearrange("a p n -> p a n"),
                         sem="DLWA" if p == 0 else "DLWB")
                wait("tensor", "DLWA" if p == 0 else "DLWB", dd)
                if half == 0:
                    use_bank(6)
                for kk in range(8):
                    k = half * 8 + kk
                    r = mm(ps[6][0:1, 0:512], SC_[:, k:k + 1],
                           WOS(p)[:, kk * 512:(kk + 1) * 512],
                           start=(k == 0), stop=(k == KD - 1),
                           inc="TV" if kk == 7 else None)
                mod_free[p] = ("TV", r)
            wait("vector", "TV", r)
            ev = vop(lambda e: e.tensor_copy(R0, ps[6][0:1, 0:512]), inc="VV")
            bank_free[6] = ("VV", ev)
            wait("tensor", "VV", ev)
            for u in range(4):
                j = jt * 4 + u
                use_bank(7)
                r2 = mm(ps[7][:, 0:1], R0[0:1, u * 128:(u + 1) * 128],
                        ONES[0:1, 0:1], start=True, stop=True, inc="TV")
                wait("vector", "TV", r2)
                ev2 = vop(lambda e, j=j: e.tensor_copy(MOD[:, j:j + 1],
                                                       ps[7][:, 0:1]), inc="VV")
                bank_free[7] = ("VV", ev2)
        vop(lambda e: e.tensor_scalar_add(MOD[:, 16:32], MOD[:, 16:32], 1.0))
        vop(lambda e: e.tensor_scalar_add(MOD[:, 64:80], MOD[:, 64:80], 1.0))

        ga_ = MOD[:, 32:48]
        gb_ = MOD[:, 80:96]
        vop(lambda e: e.tensor_scalar_max(TT1, ga_, 0.25))
        vop(lambda e: e.tensor_scalar_max(TT4, gb_, 0.25))
        vop(lambda e: e.tensor_scalar_min(TT2, ga_, -0.25))
        vop(lambda e: e.tensor_scalar_min(TT5, gb_, -0.25))
        vop(lambda e: e.tensor_single_scalar(TT3, ga_, 0.0, ALU.is_ge))
        vop(lambda e: e.tensor_single_scalar(TT6, gb_, 0.0, ALU.is_ge))
        vop(lambda e: e.tensor_sub(TT1, TT1, TT2))
        vop(lambda e: e.tensor_sub(TT4, TT4, TT5))
        vop(lambda e: e.tensor_mul(TT1, TT1, TT3))
        vop(lambda e: e.tensor_mul(TT4, TT4, TT6))
        vop(lambda e: e.tensor_add(ga_, TT1, TT2))
        vop(lambda e: e.tensor_add(gb_, TT4, TT5))
        v_mod = vop(lambda e: e.engine_nop(), inc="VV")
        if tap == "mod":
            wait("sync", "VV", v_mod)
            finish(MOD, 96)
            return nc

        prep_state = {"i": 0, "free": [mod_free[1]], "vv": 0, "pend": None}
        prep_mats = [wupr, wgater, wdownr]

        def emit_prep_consume():
            if prep_state["pend"] is None:
                return
            dd, mi = prep_state["pend"]
            prep_state["pend"] = None
            wait("scalar", "DLF", dd)
            if prep_state["vv"]:
                wait("scalar", "VV", prep_state["vv"])
            wait2("scalar", bank_free[7])
            sr = None
            for u in range(4):
                sr = sop(lambda e, u=u: e.activation(
                    ps[7][:, 0:512], PREP[:, u * 512:(u + 1) * 512], AF.Abs,
                    accum_out=ROWT[:, u:u + 1]),
                    inc="SC" if u == 3 else None)
            wait("vector", "SC", sr)
            av = None
            for u in range(4):
                av = vop(lambda e, mi=mi, u=u: e.tensor_add(
                    ABSC[:, mi * 4 + u:mi * 4 + u + 1],
                    ABSC[:, mi * 4 + u:mi * 4 + u + 1], ROWT[:, u:u + 1]),
                    inc="VV" if u == 3 else None)
            prep_state["free"] = [("SC", sr)]
            prep_state["vv"] = av
            bank_free[7] = ("SC", sr)

        def emit_prepass(n):
            for _ in range(n):
                emit_prep_consume()
                i = prep_state["i"]
                if i >= 3 * KE:
                    return
                prep_state["i"] += 1
                mi, ec = i // KE, i % KE
                wait2("sync", *prep_state["free"])
                dd = dma(PREP.rearrange("p (a n) -> p a n", n=128),
                         prep_mats[mi][ec].rearrange("a p n -> p a n"), sem="DLF")
                prep_state["pend"] = (dd, mi)

        def emit_scales():
            wait("tensor", "VV", prep_state["vv"])
            for m_ in range(3):
                use_bank(7)
                r = mm(ps[7][0:1, 0:4], ONES[:, 0:1], ABSC[:, m_ * 4:(m_ + 1) * 4],
                       start=True, stop=True, inc="TV")
                wait("vector", "TV", r)
                ev = vop(lambda e, m_=m_: e.tensor_copy(R2[0:1, m_ * 4:m_ * 4 + 4],
                                                        ps[7][0:1, 0:4]), inc="VV")
                bank_free[7] = ("VV", ev)
            vdrain()
            for m_ in range(3):
                vop(lambda e, m_=m_: e.tensor_reduce(
                    R1[0:1, m_:m_ + 1], R2[0:1, m_ * 4:m_ * 4 + 4],
                    mybir.AxisListType.X, ALU.add))
            vdrain()
            vop(lambda e: e.tensor_scalar(R1[0:1, 0:3], R1[0:1, 0:3],
                                          1.0 / (DF * E), Q_EPS, ALU.mult, ALU.add))
            vdrain()
            vop(lambda e: e.tensor_scalar_mul(R1[0:1, 3:6], R1[0:1, 0:3], 0.5))
            vdrain()
            vop(lambda e: e.tensor_scalar_mul(R1[0:1, 6:9], R1[0:1, 3:6], -1.0))
            vop(lambda e: e.tensor_mul(R1[0:1, 9:10], R1[0:1, 0:1], R1[0:1, 1:2]))
            vdrain()
            vr = vop(lambda e: e.tensor_mul(R1[0:1, 9:10], R1[0:1, 9:10],
                                            R1[0:1, 2:3]), inc="VV")
            wait("tensor", "VV", vr)
            use_bank(7)
            r = mm(ps[7][:, 0:10], ONES[0:1, :], R1[0:1, 0:10],
                   start=True, stop=True, inc="TV")
            wait("vector", "TV", r)
            vop(lambda e: e.tensor_copy(SCOL[:, 0:10], ps[7][:, 0:10]))
            vdrain()
            gv = vop(lambda e: e.tensor_scalar(GBS, MOD[:, 80:96], SCOL[:, 9:10],
                                               None, ALU.mult), inc="VV")
            bank_free[7] = ("VV", gv)
            return gv

        xw_t = xwin.rearrange("(a p) d -> a p d", p=128)
        w2c_q = wq2.rearrange("p (c n) -> c p n", n=128)
        w2c_k = wk2.rearrange("p (c n) -> c p n", n=128)
        wq1t = wq1.rearrange("(k p) m -> k p m", p=128)
        wk1t = wk1.rearrange("(k p) m -> k p m", p=128)
        xtmp_free = mod_free[1]
        x1t_free = [("VV", 0), ("VV", 0)]
        w1tmp_free = ("VV", 0)
        w2tmp_free = [("VV", 0)]
        w2p_free = [("TV", 0), ("TV", 0)]
        wvh_free = [("TV", 0), ("TV", 0)]
        wvraw_free = [mod_free[0], mod_free[1]]
        wos_free = [("TV", 0), ("TV", 0)]
        rwo = None
        for g in range(NG):
            own_lo = HALO if g == 0 else 0
            n_own = GT - own_lo
            own_glob = g * GT + own_lo - HALO
            prep_state["free"] = prep_state["free"] + [wos_free[1]]
            last_ev = 0
            rlast = None
            for tt in range(GT // 128):
                wait2("sync", xtmp_free,
                      *([wos_free[1]] if tt == 0 else []))
                dd = dma(XTMP, xw_t[(g * GT) // 128 + tt], sem="DLP")
                wait("tensor", "DLP", dd)
                for k in range(KD):
                    bank = 2 + (k % 2)
                    use_bank(bank)
                    rlast = tr(ps[bank][:, 0:128], XTMP[:, k * 128:(k + 1) * 128],
                               inc="TV")
                    wait("vector", "TV", rlast)
                    last_ev = vop(lambda e, k=k, tt=tt, b=bank: e.tensor_copy(
                        XFM(k)[:, tt * 128:(tt + 1) * 128], ps[b][:, 0:128]),
                        inc="VV")
                    bank_free[bank] = ("VV", last_ev)
                xtmp_free = ("TV", rlast)
                emit_prepass(1)
            wait("scalar", "VV", last_ev)
            sq_done = None
            for k in range(KD):
                sq_done = sop(lambda e, k=k: e.activation(
                    SQSC(k), XFM(k), AF.Square), inc="SC" if k == KD - 1 else None)
            wait("tensor", "SC", sq_done)
            use_bank(4)
            use_bank(5)
            r = None
            for k in range(KD):
                mm(ps[4][0:1, 0:GT], ONES[:, 0:1], XFM(k),
                   start=(k == 0), stop=(k == KD - 1))
                r = mm(ps[5][0:1, 0:GT], ONES[:, 0:1], SQSC(k),
                       start=(k == 0), stop=(k == KD - 1),
                       inc="TV" if k == KD - 1 else None)
            wait("vector", "TV", r)
            vop(lambda e: e.tensor_scalar_mul(R0[0:1, 0:GT], ps[4][0:1, 0:GT], 1.0 / E))
            vop(lambda e: e.tensor_scalar_mul(R1[0:1, 0:GT], ps[5][0:1, 0:GT], 1.0 / E))
            vop(lambda e: e.tensor_mul(R2[0:1, 0:GT], R0[0:1, 0:GT], R0[0:1, 0:GT]))
            vdrain()
            vop(lambda e: e.tensor_sub(R1[0:1, 0:GT], R1[0:1, 0:GT], R2[0:1, 0:GT]))
            vdrain()
            vst = vop(lambda e: e.tensor_scalar_add(R1[0:1, 0:GT], R1[0:1, 0:GT],
                                                    LN_EPS), inc="VV")
            bank_free[4] = ("VV", vst)
            bank_free[5] = ("VV", vst)
            wait("scalar", "VV", vst)
            ssq = sop(lambda e: e.activation(R1[0:1, 0:GT], R1[0:1, 0:GT], AF.Sqrt),
                      inc="SC")
            wait("vector", "SC", ssq)
            vop(lambda e: e.reciprocal(R1[0:1, 0:GT], R1[0:1, 0:GT]))
            vdrain()
            vst2 = vop(lambda e: e.tensor_mul(R2[0:1, 0:GT], R0[0:1, 0:GT],
                                              R1[0:1, 0:GT]), inc="VV")
            wait("tensor", "VV", vst2)
            for (row, dst) in [(R1, ISTD), (R2, MISTD)]:
                use_bank(6)
                r = mm(ps[6][:, 0:GT], ONES[0:1, :], row[0:1, 0:GT],
                       start=True, stop=True, inc="TV")
                wait("vector", "TV", r)
                ev = vop(lambda e, d=dst: e.tensor_copy(d, ps[6][:, 0:GT]), inc="VV")
                bank_free[6] = ("VV", ev)
            wait("vector", "VV", v_mod)
            h1v = None
            for k in range(KD):
                vop(lambda e, k=k: e.tensor_mul(SQSC(k), XFM(k), ISTD))
            for k in range(KD):
                vop(lambda e, k=k: e.tensor_sub(SQSC(k), SQSC(k), MISTD))
            for k in range(KD):
                vop(lambda e, k=k: e.tensor_scalar(
                    SQSC(k), SQSC(k), MOD[:, 16 + k:17 + k], MOD[:, k:k + 1],
                    ALU.mult, ALU.add))
            for k in range(KD):
                vop(lambda e, k=k: e.tensor_copy(H1R(k), SQSC(k)))
            for k in range(KD):
                h1v = vop(lambda e, k=k: e.tensor_sub(
                    H1LO(k), SQSC(k), H1R(k).bitcast(F32)),
                    inc="VV" if k == KD - 1 else None)
            if tap == "h1" and g == 0:
                wait("sync", "VV", h1v)
                finish(A2[:, 0:2 * KD * GT].bitcast(F32), 2 * KD * GT)
                return nc
            wait("vector", "VV", h1v)
            rg = None
            for (mat_t, dst_r, dst_lo, bank) in [(wq1t, G1QR, G1QLO, 2),
                                                 (wk1t, G1KR, G1KLO, 3)]:
                wv1 = None
                wait2("sync", wvh_free[0], wvh_free[1], ("TV", rg or 0))
                for k in range(KD):
                    wait2("sync", w1tmp_free)
                    dd = dma(W1TMP, mat_t[k], sem="DLX")
                    wait("vector", "DLX", dd)
                    if rg:
                        wait("vector", "TV", rg)
                    vop(lambda e, k=k: e.tensor_copy(W1Rv(k), W1TMP))
                    wv1 = vop(lambda e, k=k: e.tensor_sub(
                        W1LOv(k), W1TMP, W1Rv(k).bitcast(F32)), inc="VV")
                    w1tmp_free = ("VV", wv1)
                    emit_prepass(1)
                wait("tensor", "VV", wv1)
                use_bank(bank)
                for k in range(KD):
                    mm(ps[bank][:, 0:GT], W1Rv(k), H1R(k), start=(k == 0), stop=False)
                    mm(ps[bank][:, 0:GT], W1LOv(k), H1R(k), start=False, stop=False)
                    rg = mm(ps[bank][:, 0:GT], W1Rv(k), H1LO(k), start=False,
                            stop=(k == KD - 1), inc="TV" if k == KD - 1 else None)
                wait("vector", "TV", rg)
                vop(lambda e, d=dst_r, b=bank: e.tensor_copy(d, ps[b][:, 0:GT]))
                vdrain()
                g1d = vop(lambda e, dr=dst_r, dl=dst_lo, b=bank: e.tensor_sub(
                    dl, ps[b][:, 0:GT], dr.bitcast(F32)), inc="VV")
                bank_free[bank] = ("VV", g1d)
            wvh_free = [("TV", rg), ("TV", rg)]
            def emit_wv_dma(cc2):
                p2_ = cc2 % 2
                wait2("sync", wvraw_free[p2_])
                return dma(WVRAW(p2_).rearrange("p (a n) -> p a n", n=128),
                           wvr[cc2].rearrange("a p n -> p a n"),
                           sem="DLV0" if p2_ == 0 else "DLV1")

            def emit_w2_dma(cc2):
                wait2("sync", w2tmp_free[0])
                dma(W2TMP[:, 0:128], w2c_q[cc2], sem="DL2")
                return dma(W2TMP[:, 128:256], w2c_k[cc2], sem="DL2")

            d_wv = emit_wv_dma(0)
            d_w2 = emit_w2_dma(0)
            wait("vector", "DLV0", d_wv)
            wait2("vector", wvh_free[0])
            wait2("vector", wvh_free[1])
            wait("vector", "DL2", d_w2)
            wait2("vector", w2p_free[0])
            vop(lambda e: e.tensor_copy(WVHr(0), WVRAW(0)[:, 0:1024]))
            vop(lambda e: e.tensor_copy(WVHr(1), WVRAW(0)[:, 1024:2048]))
            vop(lambda e: e.tensor_copy(W2Rv(0), W2TMP))
            s0 = vop(lambda e: e.tensor_sub(WVHlo(0), WVRAW(0)[:, 0:1024],
                                            WVHr(0).bitcast(F32)), inc="VV")
            s1 = vop(lambda e: e.tensor_sub(WVHlo(1), WVRAW(0)[:, 1024:2048],
                                            WVHr(1).bitcast(F32)), inc="VV")
            w2v0 = vop(lambda e: e.tensor_sub(W2LOv(0), W2TMP,
                                              W2Rv(0).bitcast(F32)), inc="VV")
            spl_v = [s0, s1]
            w2split_v = w2v0
            wvraw_free[0] = ("VV", s1)
            w2tmp_free[0] = ("VV", w2v0)
            qv = None
            ss_last = None
            v_scan = 0
            for cc in range(KC):
                p = cc % 2
                pn = (cc + 1) % 2
                last_cc = (cc == KC - 1)
                if not last_cc:
                    d_wv = emit_wv_dma(cc + 1)
                    d_w2 = emit_w2_dma(cc + 1)
                use_bank(p)
                rvh = [None, None]
                for hh in range(2):
                    wait("tensor", "VV", spl_v[hh])
                    for kk in range(8):
                        k = hh * 8 + kk
                        first = (hh == 0 and kk == 0)
                        last = (hh == 1 and kk == 7)
                        mm(ps[p][:, 0:GT], WVHr(hh)[:, kk * 128:(kk + 1) * 128],
                           H1R(k), start=first, stop=False)
                        mm(ps[p][:, 0:GT], WVHlo(hh)[:, kk * 128:(kk + 1) * 128],
                           H1R(k), start=False, stop=False)
                        rvh[hh] = mm(ps[p][:, 0:GT],
                                     WVHr(hh)[:, kk * 128:(kk + 1) * 128],
                                     H1LO(k), start=False, stop=last,
                                     inc="TV" if kk == 7 else None)
                wait("tensor", "VV", w2split_v)
                use_bank(2)
                mm(ps[2][:, 0:GT], W2Rv(p)[:, 128:256], G1KR, start=True, stop=False)
                mm(ps[2][:, 0:GT], W2LOv(p)[:, 128:256], G1KR, start=False, stop=False)
                rk = mm(ps[2][:, 0:GT], W2Rv(p)[:, 128:256], G1KLO,
                        start=False, stop=True, inc="TV")
                use_bank(3)
                mm(ps[3][:, 0:n_own], W2Rv(p)[:, 0:128], G1QR[:, own_lo:GT],
                   start=True, stop=False)
                mm(ps[3][:, 0:n_own], W2LOv(p)[:, 0:128], G1QR[:, own_lo:GT],
                   start=False, stop=False)
                rq = mm(ps[3][:, 0:n_own], W2Rv(p)[:, 0:128], G1QLO[:, own_lo:GT],
                        start=False, stop=True, inc="TV")
                w2p_free[p] = ("TV", rq)
                wait("scalar", "TV", rk)
                if v_scan:
                    wait("scalar", "VV", v_scan)
                sop(lambda e: e.activation(FB, ps[2][:, 0:GT], AF.Sigmoid))
                wait("scalar", "TV", rq)
                sqq_prev = qv
                if sqq_prev:
                    wait("scalar", "VV", sqq_prev)
                sop(lambda e, n=n_own: e.activation(
                    SQB[:, 0:n], ps[3][:, 0:n], AF.Sigmoid))
                sa = sop(lambda e: e.activation(AB, FB, AF.Sigmoid), inc="SC")
                wait("vector", "TV", rvh[0])
                wait("vector", "SC", sa)
                vop(lambda e: e.tensor_scalar(KB, AB, -1.0, 1.0, ALU.mult, ALU.add))
                if not last_cc:
                    wait("vector", "DLV0" if pn == 0 else "DLV1", d_wv)
                    vop(lambda e, pn=pn: e.tensor_copy(
                        WVHr(0), WVRAW(pn)[:, 0:1024]))
                else:
                    vdrain()
                wait("vector", "TV", rvh[1])
                vop(lambda e, p=p: e.tensor_mul(BB, KB, ps[p][:, 0:GT]))
                if not last_cc:
                    wait("vector", "TV", rvh[1])
                    vop(lambda e, pn=pn: e.tensor_copy(
                        WVHr(1), WVRAW(pn)[:, 1024:2048]))
                else:
                    vdrain()
                vb = None
                if g == 0:
                    vb = vop(lambda e: e.tensor_mul(BB, BB, MASK), inc="VV")
                if not last_cc:
                    wait("vector", "DL2", d_w2)
                    wait2("vector", w2p_free[pn])
                    vop(lambda e, pn=pn: e.tensor_copy(W2Rv(pn), W2TMP))
                else:
                    vdrain()
                vop(lambda e, cc=cc: e.tensor_tensor_scan(
                    HB, AB, BB, HCAR[:, cc:cc + 1], ALU.mult, ALU.add))
                if vb is None:
                    vb = C["VV"]
                if not last_cc:
                    w0 = vop(lambda e, pn=pn: e.tensor_sub(
                        WVHlo(0), WVRAW(pn)[:, 0:1024], WVHr(0).bitcast(F32)),
                        inc="VV")
                else:
                    vdrain()
                    w0 = None
                v_scan = vop(lambda e, cc=cc: e.tensor_copy(
                    HCAR[:, cc:cc + 1], HB[:, GT - 1:GT]), inc="VV")
                bank_free[p] = ("VV", v_scan)
                sqa = vop(lambda e, n=n_own: e.tensor_mul(
                    SQB[:, 0:n], SQB[:, 0:n], ps[3][:, 0:n]), inc="VV")
                bank_free[3] = ("VV", sqa)
                if not last_cc:
                    w1 = vop(lambda e, pn=pn: e.tensor_sub(
                        WVHlo(1), WVRAW(pn)[:, 1024:2048], WVHr(1).bitcast(F32)),
                        inc="VV")
                    wvraw_free[pn] = ("VV", w1)
                else:
                    vdrain()
                    w1 = None
                vop(lambda e, n=n_own, o=own_lo: e.tensor_mul(
                    SQB[:, 0:n], SQB[:, 0:n], HB[:, o:GT]))
                if not last_cc:
                    w2s = vop(lambda e, pn=pn: e.tensor_sub(
                        W2LOv(pn), W2TMP, W2Rv(pn).bitcast(F32)), inc="VV")
                    w2tmp_free[0] = ("VV", w2s)
                    w2split_v = w2s
                else:
                    vdrain()
                qv = vop(lambda e, cc=cc, n=n_own: e.tensor_scalar(
                    QH(cc, n), SQB[:, 0:n], NW[:, cc:cc + 1], None, ALU.mult),
                    inc="VV")
                if not last_cc:
                    spl_v = [w0, w1]
                wait("scalar", "VV", qv)
                scq = sop(lambda e, n=n_own: e.activation(
                    KB[:, 0:n], SQB[:, 0:n], AF.Square), inc="SC")
                wait("tensor", "SC", scq)
                if cc == 0:
                    use_bank(4)
                ss_last = mm(ps[4][0:1, 0:n_own], ONES[:, 0:1], KB[:, 0:n_own],
                             start=(cc == 0), stop=(cc == KC - 1), inc="TV")
                emit_prepass(1)
            wvh_free = [("TV", rvh[0]), ("TV", rvh[1])]
            if tap == "scan" and g == 0:
                wait("sync", "VV", qv)
                finish(fv(A1, 0, 32 * n_own), 32 * n_own)
                return nc
            wait("vector", "TV", ss_last)
            vsr = vop(lambda e, n=n_own: e.tensor_scalar(
                R0[0:1, 0:n], ps[4][0:1, 0:n], 1.0 / E2, RMS_EPS, ALU.mult,
                ALU.add), inc="VV")
            bank_free[4] = ("VV", vsr)
            wait("scalar", "VV", vsr)
            ssq2 = sop(lambda e, n=n_own: e.activation(R0[0:1, 0:n], R0[0:1, 0:n],
                                                       AF.Sqrt), inc="SC")
            wait("vector", "SC", ssq2)
            vrr = vop(lambda e, n=n_own: e.reciprocal(R0[0:1, 0:n], R0[0:1, 0:n]),
                      inc="VV")
            wait("tensor", "VV", vrr)
            use_bank(6)
            rb = mm(ps[6][:, 0:n_own], ONES[0:1, :], R0[0:1, 0:n_own],
                    start=True, stop=True, inc="TV")
            wait("vector", "TV", rb)
            erb = vop(lambda e, n=n_own: e.tensor_copy(RSTD[:, 0:n],
                                                       ps[6][:, 0:n]), inc="VV")
            bank_free[6] = ("VV", erb)
            wait("vector", "VV", erb)
            emit_prep_consume()
            wos_req = [wvraw_free[0], wvraw_free[1], wos_free[0], wos_free[1],
                       *prep_state["free"], xtmp_free]
            for mc in range(KD):
                p2 = mc % 2
                if mc < 2:
                    wait2("sync", *wos_req)
                else:
                    wait2("sync", wos_free[p2])
                dwo = dma(WOS(p2).rearrange("p (a n) -> p a n", n=128),
                          wor[mc].rearrange("a p n -> p a n"),
                          sem="DLWA" if p2 == 0 else "DLWB")
                bank = 5 + p2
                wait("tensor", "DLWA" if p2 == 0 else "DLWB", dwo)
                use_bank(bank)
                rwo = None
                for cc in range(KC):
                    rwo = mm(ps[bank][:, 0:n_own],
                             WOS(p2)[:, cc * 128:(cc + 1) * 128], QH(cc, n_own),
                             start=(cc == 0), stop=(cc == KC - 1),
                             inc="TV" if cc == KC - 1 else None)
                wos_free[p2] = ("TV", rwo)
                wait("vector", "TV", rwo)
                wait2("vector", x1t_free[p2])
                vop(lambda e, b=bank, n=n_own, p2=p2: e.tensor_mul(
                    X1T(p2)[:, 0:n], ps[b][:, 0:n], RSTD[:, 0:n]))
                vdrain()
                vx1 = vop(lambda e, mc=mc, n=n_own, o=own_lo, p2=p2:
                          e.scalar_tensor_tensor(
                              X1T(p2)[:, 0:n], X1T(p2)[:, 0:n],
                              MOD[:, 32 + mc:33 + mc], XFM(mc)[:, o:GT],
                              ALU.mult, ALU.add), inc="VV")
                bank_free[bank] = ("VV", vx1)
                wait("sync", "VV", vx1)
                stx = dma(x1s[:, mc * OWN + own_glob:mc * OWN + own_glob + n_own],
                          X1T(p2)[:, 0:n_own], sem="STA" if p2 == 0 else "STB")
                x1t_free[p2] = ("STA" if p2 == 0 else "STB", stx)
            wvraw_free = [("TV", rwo), ("TV", rwo)]
        prep_state["free"] = prep_state["free"] + [wos_free[1]]
        emit_prepass(3 * KE)
        emit_prep_consume()
        scales_done = emit_scales()
        tv_attn = C["TV"]
        vv_attn = C["VV"]
        if tap == "x1":
            wait("sync", "VV", scales_done)
            wait("sync", "STA", C["STA"])
            wait("sync", "STB", C["STB"])
            dwb = dma(A4[:, 0:8192], x1s[:, 0:8192])
            wait("sync", "DLX", dwb)
            finish(A4[:, 0:8192], 8192)
            return nc

        x2_free = [("TV", tv_attn), ("VV", vv_attn),
                   ("STA", C["STA"]), ("STB", C["STB"])]
        wst = {"n": 0,
               "free": [[("TV", tv_attn)], [("TV", tv_attn)]],
               "half": [("TV", tv_attn), ("TV", tv_attn)]}
        hid_free = [("TV", tv_attn)] * 4
        tds_free = [("TV", tv_attn)] * 4
        dacc_v = [0] * KD
        rdn = None
        for h in range(NH):
            if isinstance(x2_free, list):
                wait2("sync", *x2_free)
            else:
                wait2("sync", x2_free)
            dx = None
            for mc in range(KD):
                dx = dma(X2H(mc), x1s[:, mc * OWN + h * HT:mc * OWN + (h + 1) * HT])
            wait("scalar", "DLX", dx)
            if h == 0:
                wait("scalar", "TV", tv_attn)
            sq_done = None
            for k in range(KD):
                sq_done = sop(lambda e, k=k: e.activation(
                    H2SQ(k), X2H(k), AF.Square), inc="SC" if k == KD - 1 else None)
            wait("tensor", "SC", sq_done)
            use_bank(4)
            use_bank(5)
            r = None
            for k in range(KD):
                mm(ps[4][0:1, 0:HT], ONES[:, 0:1], X2H(k),
                   start=(k == 0), stop=(k == KD - 1))
                r = mm(ps[5][0:1, 0:HT], ONES[:, 0:1], H2SQ(k),
                       start=(k == 0), stop=(k == KD - 1),
                       inc="TV" if k == KD - 1 else None)
            wait("vector", "TV", r)
            vop(lambda e: e.tensor_scalar_mul(R0[0:1, 0:HT], ps[4][0:1, 0:HT], 1.0 / E))
            vop(lambda e: e.tensor_scalar_mul(R1[0:1, 0:HT], ps[5][0:1, 0:HT], 1.0 / E))
            vop(lambda e: e.tensor_mul(R2[0:1, 0:HT], R0[0:1, 0:HT], R0[0:1, 0:HT]))
            vdrain()
            vop(lambda e: e.tensor_sub(R1[0:1, 0:HT], R1[0:1, 0:HT], R2[0:1, 0:HT]))
            vdrain()
            vst = vop(lambda e: e.tensor_scalar_add(R1[0:1, 0:HT], R1[0:1, 0:HT],
                                                    LN_EPS), inc="VV")
            bank_free[4] = ("VV", vst)
            bank_free[5] = ("VV", vst)
            wait("scalar", "VV", vst)
            ssq = sop(lambda e: e.activation(R1[0:1, 0:HT], R1[0:1, 0:HT], AF.Sqrt),
                      inc="SC")
            wait("vector", "SC", ssq)
            vop(lambda e: e.reciprocal(R1[0:1, 0:HT], R1[0:1, 0:HT]))
            vdrain()
            vst2 = vop(lambda e: e.tensor_mul(R2[0:1, 0:HT], R0[0:1, 0:HT],
                                              R1[0:1, 0:HT]), inc="VV")
            wait("tensor", "VV", vst2)
            for (row, dst) in [(R1, I2), (R2, M2)]:
                use_bank(6)
                r = mm(ps[6][:, 0:HT], ONES[0:1, :], row[0:1, 0:HT],
                       start=True, stop=True, inc="TV")
                wait("vector", "TV", r)
                ev = vop(lambda e, d=dst: e.tensor_copy(d, ps[6][:, 0:HT]), inc="VV")
                bank_free[6] = ("VV", ev)
            wait("vector", "VV", scales_done)
            h2v = None
            for k in range(KD):
                vop(lambda e, k=k: e.tensor_mul(H2SQ(k), X2H(k), I2))
            for k in range(KD):
                vop(lambda e, k=k: e.tensor_sub(H2SQ(k), H2SQ(k), M2))
            for k in range(KD):
                vop(lambda e, k=k: e.tensor_scalar(
                    H2SQ(k), H2SQ(k), MOD[:, 64 + k:65 + k], MOD[:, 48 + k:49 + k],
                    ALU.mult, ALU.add))
            for k in range(KD):
                vop(lambda e, k=k: e.tensor_copy(H2R(k), H2SQ(k)))
            for k in range(KD):
                h2v = vop(lambda e, k=k: e.tensor_sub(
                    H2LO(k), H2SQ(k), H2R(k).bitcast(F32)),
                    inc="VV" if k == KD - 1 else None)
            if tap == "h2" and h == 0:
                wait("sync", "VV", h2v)
                finish(A1[:, 0:8192].bitcast(F32), 8192)
                return nc
            wait2("vector", wos_free[0], wos_free[1], *prep_state["free"],
                  wvraw_free[0], wvraw_free[1])
            dz = vop(lambda e: e.memset(A4[:, 0:8192], 0.0), inc="VV")
            for mc in range(KD):
                dacc_v[mc] = dz
            def emit_tern(mat, ec, scix, dst_fn, chunk_free):
                p = wst["n"] % 2
                wst["n"] += 1
                sem = "DLF" if p == 0 else "DLF1"
                wait2("sync", *wst["free"][p])
                if p == 0:
                    dd = dma(WFL.rearrange("p (a n) -> p a n", n=128),
                             mat[ec].rearrange("a p n -> p a n"), sem=sem)
                else:
                    mr = mat[ec].rearrange("a p n -> p a n")
                    dma(WFB1.rearrange("p (a n) -> p a n", n=128),
                        mr[:, 0:8], sem=sem)
                    dd = dma(WFB2.rearrange("p (a n) -> p a n", n=128),
                             mr[:, 8:16], sem=sem)
                wait("scalar", sem, dd)
                wait("vector", sem, dd)
                tv = st = None
                for cch in range(4):
                    gh = cch % 2
                    if p == 0:
                        src = WFL[:, cch * 512:(cch + 1) * 512]
                    elif cch < 2:
                        src = WFB1[:, cch * 512:(cch + 1) * 512]
                    else:
                        src = WFB2[:, (cch - 2) * 512:(cch - 1) * 512]
                    wait2("scalar", wst["half"][gh])
                    sop(lambda e, src=src, gh=gh, scix=scix: e.activation(
                        GTSC[:, gh * 512:(gh + 1) * 512], src, AF.Sign,
                        bias=SCOL[:, scix + 3:scix + 4], scale=ONES[:, 0:1]))
                    st = sop(lambda e, gh=gh: e.activation(
                        GTSC[:, gh * 512:(gh + 1) * 512],
                        GTSC[:, gh * 512:(gh + 1) * 512], AF.Relu),
                        inc="SC")
                    vop(lambda e, src=src, gh=gh, scix=scix: e.tensor_scalar(
                        LTSC[:, gh * 512:(gh + 1) * 512], src,
                        SCOL[:, scix + 3:scix + 4], None, ALU.is_lt))
                    wait("vector", "SC", st)
                    wait2("vector", chunk_free[cch])
                    dap = dst_fn()
                    tv = vop(lambda e, cch=cch, gh=gh, dap=dap: e.tensor_sub(
                        dap[:, cch * 512:(cch + 1) * 512],
                        GTSC[:, gh * 512:(gh + 1) * 512],
                        LTSC[:, gh * 512:(gh + 1) * 512]), inc="VV")
                    wst["half"][gh] = ("VV", tv)
                wst["free"][p] = [("SC", st), ("VV", tv)]
                return tv

            wait("scalar", "VV", scales_done)
            upfree = [("VV", h2v)] * 4
            gtfree = [("VV", h2v)] * 4
            tvu_next = emit_tern(wupr, 0, 3, lambda: TUG(0), upfree)
            tvg_next = emit_tern(wgater, 0, 4, lambda: TUG(1), gtfree)
            emit_tern(wdownr, 0, 5, lambda: TDS(0), [tds_free[0]] * 4)
            for ec in range(KE):
                j = ec % 4
                bu, bg = (0, 1) if ec % 2 == 0 else (4, 5)
                tvu, tvg = tvu_next, tvg_next
                wait("tensor", "VV", tvu)
                if ec == 0:
                    wait("tensor", "VV", h2v)
                use_bank(bu)
                for k in range(KD):
                    mm(ps[bu][:, 0:HT], TUG(0)[:, k * 128:(k + 1) * 128], H2R(k),
                       start=(k == 0), stop=False)
                    r_ = mm(ps[bu][:, 0:HT], TUG(0)[:, k * 128:(k + 1) * 128],
                            H2LO(k), start=False, stop=(k == KD - 1),
                            inc="TV" if k % 4 == 3 else None)
                    if k % 4 == 3:
                        upfree[k // 4] = ("TV", r_)
                wait("tensor", "VV", tvg)
                use_bank(bg)
                ru = None
                for k in range(KD):
                    mm(ps[bg][:, 0:HT], TUG(1)[:, k * 128:(k + 1) * 128], H2R(k),
                       start=(k == 0), stop=False)
                    ru = mm(ps[bg][:, 0:HT], TUG(1)[:, k * 128:(k + 1) * 128],
                            H2LO(k), start=False, stop=(k == KD - 1),
                            inc="TV" if k % 4 == 3 else None)
                    if k % 4 == 3:
                        gtfree[k // 4] = ("TV", ru)
                if ec + 1 < KE:
                    tvu_next = emit_tern(wupr, ec + 1, 3, lambda: TUG(0),
                                         upfree)
                wait("scalar", "TV", ru)
                sg = sop(lambda e, b=bu: e.activation(
                    SGB, ps[b][:, 0:HT], AF.Sigmoid, scale=SCOL[:, 0:1]),
                    inc="SC")
                if ec + 1 < KE:
                    tvg_next = emit_tern(wgater, ec + 1, 4, lambda: TUG(1),
                                         gtfree)
                wait("vector", "SC", sg)
                wait2("vector", hid_free[j])
                vop(lambda e, b=bu: e.tensor_mul(T1B, SGB, ps[b][:, 0:HT]))
                vdrain()
                vop(lambda e, b=bg: e.tensor_mul(HPL, T1B, ps[b][:, 0:HT]))
                vdrain()
                vop(lambda e, j=j: e.tensor_copy(HIDR(j), HPL))
                vdrain()
                hv = vop(lambda e, j=j: e.tensor_sub(
                    HIDLO(j), HPL, HIDR(j).bitcast(F32)), inc="VV")
                bank_free[bu] = ("VV", hv)
                bank_free[bg] = ("VV", hv)
                if j == 3:
                    wait("tensor", "VV", hv)
                    for mc in range(KD):
                        bank = 2 + mc % 2
                        use_bank(bank)
                        for jj in range(4):
                            mm(ps[bank][:, 0:HT],
                               TDS(jj)[:, mc * 128:(mc + 1) * 128],
                               HIDR(jj), start=(jj == 0), stop=False)
                            rdn = mm(ps[bank][:, 0:HT],
                                     TDS(jj)[:, mc * 128:(mc + 1) * 128],
                                     HIDLO(jj), start=False, stop=(jj == 3),
                                     inc="TV" if jj == 3 else None)
                        wait("vector", "TV", rdn)
                        wait("vector", "VV", dacc_v[mc])
                        dacc_v[mc] = vop(lambda e, mc=mc, b=bank: e.tensor_add(
                            DACC(mc), DACC(mc), ps[b][:, 0:HT]), inc="VV")
                        bank_free[bank] = ("VV", dacc_v[mc])
                    for jj in range(4):
                        hid_free[jj] = ("TV", rdn)
                        tds_free[jj] = ("TV", rdn)
                if ec + 1 < KE:
                    emit_tern(wdownr, ec + 1, 5,
                              lambda jn=(ec + 1) % 4: TDS(jn),
                              [tds_free[(ec + 1) % 4]] * 4)
            wait("sync", "TV", rdn)
            dx2 = None
            for mc in range(KD):
                dx2 = dma(X2H(mc), x1s[:, mc * OWN + h * HT:mc * OWN + (h + 1) * HT])
            wait("vector", "DLX", dx2)
            xv = None
            for mc in range(KD):
                wait("vector", "VV", dacc_v[mc])
                xv = vop(lambda e, mc=mc: e.scalar_tensor_tensor(
                    X2H(mc), DACC(mc), GBS[:, mc:mc + 1], X2H(mc),
                    ALU.mult, ALU.add), inc="VV")
            wait("tensor", "VV", xv)
            xo_free = wst["free"][0][0]
            for tt in range(HT // 128):
                evo = None
                for mc in range(KD):
                    bank = 6 + mc % 2
                    use_bank(bank)
                    rto = tr(ps[bank][:, 0:128],
                             X2H(mc)[:, tt * 128:(tt + 1) * 128], inc="TV")
                    wait("vector", "TV", rto)
                    wait2("vector", xo_free)
                    evo = vop(lambda e, mc=mc, b=bank: e.tensor_copy(
                        XO[:, mc * 128:(mc + 1) * 128], ps[b][:, 0:128]), inc="VV")
                    bank_free[bank] = ("VV", evo)
                wait("sync", "VV", evo)
                row0 = h * HT + tt * 128
                dma(xout[row0:row0 + 128, :], XO, sem="STO")
                xo_free = ("STO", C["STO"])
            x2_free = [("STO", C["STO"]), ("STA", C["STA"]), ("STB", C["STB"])]
            wst["free"][0] = [("STO", C["STO"])]

        finish()
        return nc


def emit_all(block, STR):
    @block.sync
    def _(sync):
        for f in STR["sync"]:
            f(sync)

    @block.tensor
    def _(tensor):
        for f in STR["tensor"]:
            f(tensor)

    @block.vector
    def _(vector):
        for f in STR["vector"]:
            f(vector)

    @block.scalar
    def _(scalar):
        for f in STR["scalar"]:
            f(scalar)


_NC_CACHE = {}


def _get_nc(tap=None):
    if tap not in _NC_CACHE:
        _NC_CACHE[tap] = build_nc(tap)
    return _NC_CACHE[tap]


def stage_inputs(inputs):
    f32 = np.float32
    x = np.ascontiguousarray(np.asarray(inputs["L0_attn_x"], dtype=f32))
    c = np.ascontiguousarray(np.asarray(inputs["L0_c"], dtype=f32))

    wv_r = np.ascontiguousarray(np.asarray(
        inputs["L0_attn_Wv"], dtype=f32).reshape(KD, 128, KC, 128).transpose(2, 0, 1, 3))
    wo_r = np.ascontiguousarray(np.asarray(
        inputs["L0_attn_Wo"], dtype=f32).reshape(KC, 128, KD, 128).transpose(2, 0, 1, 3))
    wup_r = np.ascontiguousarray(np.asarray(
        inputs["L0_ffn_W_up"], dtype=f32).T.reshape(KD, 128, KE, 128).transpose(2, 0, 1, 3))
    wgate_r = np.ascontiguousarray(np.asarray(
        inputs["L0_ffn_W_gate"], dtype=f32).T.reshape(KD, 128, KE, 128).transpose(2, 0, 1, 3))
    wdown_r = np.ascontiguousarray(np.asarray(
        inputs["L0_ffn_W_down"], dtype=f32).T.reshape(KE, 128, KD, 128).transpose(0, 2, 1, 3))
    wmod_r = np.ascontiguousarray(np.asarray(
        inputs["L0_wmod"], dtype=f32).reshape(KD, 128, 24, 512).transpose(2, 0, 1, 3))
    normw_r = np.ascontiguousarray(np.asarray(
        inputs["L0_attn_norm_w"], dtype=f32).reshape(KC, 128).T)
    idt = np.eye(128, dtype=f32)
    wq1 = np.ascontiguousarray(np.asarray(inputs["L0_attn_Wq1"], dtype=f32))
    wk1 = np.ascontiguousarray(np.asarray(inputs["L0_attn_Wk1"], dtype=f32))
    wq2 = np.ascontiguousarray(np.asarray(inputs["L0_attn_Wq2"], dtype=f32))
    wk2 = np.ascontiguousarray(np.asarray(inputs["L0_attn_Wk2"], dtype=f32))

    in_maps = []
    for core in range(8):
        b, q = core // 4, core % 4
        if q == 0:
            xw = np.concatenate([x[b, 0:HALO], x[b, 0:OWN]], axis=0)
            mk = np.ones((1, GT), dtype=f32)
            mk[0, 0:HALO] = 0.0
        else:
            xw = x[b, q * OWN - HALO:(q + 1) * OWN]
            mk = np.ones((1, GT), dtype=f32)
        mask = np.ascontiguousarray(np.broadcast_to(mk, (128, GT)))
        cre = np.ascontiguousarray(c[b].reshape(KD, 128).T)
        in_maps.append({
            "xwin": np.ascontiguousarray(xw), "maskr": mask, "cresh": cre,
            "wq1": wq1, "wk1": wk1, "wq2": wq2, "wk2": wk2,
            "wvr": wv_r, "wor": wo_r, "wupr": wup_r, "wgater": wgate_r,
            "wdownr": wdown_r, "wmodr": wmod_r, "normwr": normw_r,
            "idt_in": idt,
        })
    return in_maps


def kernel(**inputs):
    in_maps = stage_inputs(inputs)
    nc = _get_nc()
    res = run_bass_kernel_spmd(nc, in_maps, core_ids=list(range(8)))
    out = np.empty((2, 4096, E), dtype=np.float32)
    for core in range(8):
        b, q = core // 4, core % 4
        out[b, q * OWN:(q + 1) * OWN, :] = res.results[core]["xout"]
    return out
